# version 06
# baseline (speedup 1.0000x reference)
import numpy as np
import ml_dtypes
import concourse.bass as bass
import concourse.mybir as mybir
from concourse.bass_utils import run_bass_kernel_spmd

F32 = mybir.dt.float32
BF16 = mybir.dt.bfloat16
I8 = mybir.dt.int8
AF = mybir.ActivationFunctionType
OP = mybir.AluOpType

L, D, E, N, DT, K = 4096, 1024, 2048, 16, 64, 4
CH = 512
NCH = L // CH
ET, DTl = E // 128, D // 128
TT = CH // 128
MAGIC = float(np.float32(12582912.0))
LN_EPS = 1e-5
RMS_EPS = 1e-5
Q_EPS = 1e-6


def _act_set_idx(nc, *funcs):
    from concourse.hw_specs import get_activation_tables
    tabs = get_activation_tables(nc.m.arch)
    for idx, (name, fs) in enumerate(tabs.items()):
        if all(f in fs for f in funcs):
            return idx
    raise ValueError(f"no table set with {funcs}")


class Plan:
    ENG = ("sync", "gpsimd", "vector", "scalar", "tensor")

    def __init__(self, semnames):
        self.ops = {e: [] for e in self.ENG}
        self.cnt = {s: 0 for s in semnames}
        self.sems = None

    def emit(self, eng, fn, inc=None):
        if inc is not None:
            sem, amt = inc
            self.cnt[sem] += amt
            def wrapped(e, fn=fn, sem=sem, amt=amt):
                fn(e).then_inc(self.sems[sem], amt)
            self.ops[eng].append(wrapped)
            return self.cnt[sem]
        self.ops[eng].append(fn)
        return None

    def wait(self, eng, sem, val):
        if val <= 0:
            return
        assert val <= self.cnt[sem], (eng, sem, val, self.cnt[sem])
        self.ops[eng].append(lambda e, s=sem, v=val: e.wait_ge(self.sems[s], v))

    def cur(self, sem):
        return self.cnt[sem]


def build_core(dbg_tags=()):
    import os
    NCH_RUN = int(os.environ.get("MK_NCH", NCH))
    PH = int(os.environ.get("MK_PH", 9))
    OF = int(os.environ.get("MK_OF", 9))
    PH5 = int(os.environ.get("MK_PH5", 9))
    NOWO = os.environ.get("MK_NOWO", "0") == "1"
    NOOMM = os.environ.get("MK_NOOMM", "0") == "1"
    nc = bass.Bass("TRN2", target_bir_lowering=False,
                   detect_race_conditions=False)
    SET_EXPLN = _act_set_idx(nc, AF.Exp, AF.Ln)
    SET_SILU = _act_set_idx(nc, AF.Silu)

    x_in = nc.dram_tensor("x", [L, D], F32, kind="ExternalInput").ap()
    y_out = nc.dram_tensor("y", [L, D], F32, kind="ExternalOutput").ap()
    w_in = nc.dram_tensor("w_in", [D, E], F32, kind="ExternalInput").ap()
    w_skip = nc.dram_tensor("w_skip", [ET, D, 128], F32, kind="ExternalInput").ap()
    w_out = nc.dram_tensor("w_out", [ET, 128, D], F32, kind="ExternalInput").ap()
    w_q = nc.dram_tensor("w_q", [D, D], I8, kind="ExternalInput").ap()
    wd1t = nc.dram_tensor("wd1t", [128, ET * DT], F32, kind="ExternalInput").ap()
    wd2t = nc.dram_tensor("wd2t", [DT, E], F32, kind="ExternalInput").ap()
    wbt = nc.dram_tensor("wbt", [128, ET * N], F32, kind="ExternalInput").ap()
    wct = nc.dram_tensor("wct", [128, ET * N], F32, kind="ExternalInput").ap()
    alog = nc.dram_tensor("alog", [128, ET * N], F32, kind="ExternalInput").ap()
    normw = nc.dram_tensor("normw", [128, DTl], F32, kind="ExternalInput").ap()
    convw = nc.dram_tensor("convw", [128, ET * K], F32, kind="ExternalInput").ap()
    convb = nc.dram_tensor("convb", [128, ET], F32, kind="ExternalInput").ap()
    wd2b = nc.dram_tensor("wd2b", [128, ET], F32, kind="ExternalInput").ap()
    wdvec = nc.dram_tensor("wdvec", [128, ET], F32, kind="ExternalInput").ap()
    fbias = nc.dram_tensor("fbias", [128, DTl], F32, kind="ExternalInput").ap()
    fws = nc.dram_tensor("fws", [128, DTl], F32, kind="ExternalInput").ap()

    wc_in = nc.dram_tensor("wc_in", [4, DTl, 128, 512], BF16, kind="Internal").ap()
    wc_skip = nc.dram_tensor("wc_skip", [ET, DTl, 128, 128], BF16, kind="Internal").ap()
    wc_out = nc.dram_tensor("wc_out", [ET, 128, D], BF16, kind="Internal").ap()
    wc_wq = nc.dram_tensor("wc_wq", [2, DTl, 128, 512], BF16, kind="Internal").ap()

    ident_c = nc.inline_tensor(np.eye(128, dtype=np.float32), "identf_c").ap()
    onesb_c = nc.inline_tensor(np.ones((1, 128), dtype=ml_dtypes.bfloat16), "onesb_c").ap()
    _sel = np.zeros((16, 16 * 128), dtype=ml_dtypes.bfloat16)
    for _n in range(16):
        _sel[_n, _n * 128:(_n + 1) * 128] = 1.0
    selb_c = nc.inline_tensor(_sel, "selb_c").ap()
    onesf_c = nc.inline_tensor(np.ones((1, 128), dtype=np.float32), "onesf_c").ap()

    dbg = {}
    for tag, shape in dbg_tags:
        dbg[tag] = nc.dram_tensor("dbg_" + tag, list(shape), F32,
                                  kind="ExternalOutput").ap()

    from contextlib import ExitStack
    with ExitStack() as stack:
        sbt = lambda n, s, d: stack.enter_context(nc.sbuf_tensor(n, s, d))
        pst = lambda n: stack.enter_context(nc.psum_tensor(n, [128, 512], F32))

        wd1_bf = sbt("wd1_bf", [128, ET * DT], BF16)
        wd2_bf = sbt("wd2_bf", [DT, E], BF16)
        wb_bf = sbt("wb_bf", [128, ET * N], BF16)
        wc_bf = sbt("wc_bf", [128, ET * N], BF16)
        acol = sbt("acol", [128, ET * N], F32)
        normw_c = sbt("normw_c", [128, DTl], F32)
        convb_c = sbt("convb_c", [128, ET], F32)
        wd2b_c = sbt("wd2b_c", [128, ET], F32)
        fbias_c = sbt("fbias_c", [128, DTl], F32)
        fws_c = sbt("fws_c", [128, DTl], F32)
        convw_c = sbt("convw_c", [128, ET * K], F32)
        wdvec_c = sbt("wdvec_c", [128, ET], F32)
        identf = sbt("identf", [128, 128], F32)
        identb = sbt("identb", [128, 128], BF16)
        onesb = sbt("onesb", [1, 128], BF16)
        selb = sbt("selb", [16, 16 * 128], BF16)
        onesf = sbt("onesf", [1, 128], F32)
        cdiag = sbt("cdiag", [128, ET * K * 128], BF16)
        wdiag = sbt("wdiag", [128, ET * 128], BF16)
        hstate = sbt("hstate", [128, ET * N], F32)
        halo = sbt("halo", [128, ET * 3], BF16)
        epsc = sbt("epsc", [128, 1], F32)

        win_q = sbt("win_q", [128, 2 * DTl * 512], BF16)
        wsk_t = sbt("wsk_t", [128, 4 * DTl * 128], BF16)
        wo_t = sbt("wo_t", [128, 4 * D], BF16)
        wq_t = sbt("wq_t", [128, 8 * 512], BF16)

        xbuf = sbt("xbuf", [128, 2 * TT * D], F32)
        sqtmp = sbt("sqtmp", [128, D], F32)
        xnsf = sbt("xnsf", [128, D], F32)
        xnsT = sbt("xnsT", [128, DTl * 512], BF16)
        xqT = sbt("xqT", [128, DTl * 512], BF16)
        xact = sbt("xact", [128, ET * 512], BF16)
        ss_bf = sbt("ss_bf", [128, ET * 512], BF16)
        xtl = sbt("xtl", [128, 2 * 515], BF16)
        dl = sbt("dl", [128, 4 * 512], F32)
        dx = sbt("dx", [128, 4 * 512], BF16)
        t1f = sbt("t1f", [128, 2 * 512], F32)
        abar = sbt("abar", [128, 2 * 512], F32)
        ubuf = sbt("ubuf", [128, 2 * 512], BF16)
        wbuf = sbt("wbuf", [128, 4 * 512], BF16)
        brep = sbt("brep", [128, 2 * 512], BF16)
        crep = sbt("crep", [128, 2 * 512], BF16)
        btsb = sbt("btsb", [16, 512], BF16)
        ctsb = sbt("ctsb", [16, 512], BF16)
        z1bf = sbt("z1bf", [DT, 512], BF16)
        xsr = sbt("xsr", [1, TT * 128], F32)
        xsrep = sbt("xsrep", [128, 512], F32)
        mxs = sbt("mxs", [128, 2 * 512], F32)
        statc = sbt("statc", [128, TT * 16], F32)

        pb = [pst(f"pb{i}") for i in range(8)]

        semnames = (["sq", "gq", "sv", "ss", "st", "sg", "sqp", "sox", "sx0", "sx1"]
                    + [f"win{i}" for i in range(2)]
                    + [f"wsk{i}" for i in range(4)]
                    + [f"wo{i}" for i in range(4)]
                    + [f"wq{i}" for i in range(8)])
        p = Plan(semnames)
        sems = {}
        for sname in semnames:
            sems[sname] = stack.enter_context(nc.semaphore(sname))
        p.sems = sems

        V, S, T, SY, G = "vector", "scalar", "tensor", "sync", "gpsimd"

        def _attach(eng, sem):
            fn = p.ops[eng].pop()
            p.cnt[sem] += 1
            def wrapped(e, fn=fn, sem=sem):
                fn(e).then_inc(p.sems[sem], 1)
            p.ops[eng].append(wrapped)
            return p.cnt[sem]

        def v_inc():
            return _attach(V, "sv")

        def s_inc():
            return _attach(S, "ss")


        def sc(col, tt):
            return statc[:, tt * 16 + col:tt * 16 + col + 1]

        def xba(par, tt):
            return xbuf[:, (par * TT + tt) * D:(par * TT + tt + 1) * D]

        def dump(tag, ap, block=()):
            if tag not in dbg:
                return
            for s in ("sv", "ss", "st"):
                p.wait(G, s, p.cur(s))
            p.emit(G, lambda e, ap=ap, tag=tag: e.dma_start(
                out=dbg[tag], in_=ap), inc=("gq", 16))
            p.wait(G, "gq", p.cur("gq"))
            for eng in block:
                p.wait(eng, "gq", p.cur("gq"))

        wd1_st = xbuf[:, 0:ET * DT]
        wd2_st = xbuf[0:DT, 2048:2048 + E]
        wb_st = xbuf[:, 4096:4096 + ET * N]
        wc_st = xbuf[:, 4352:4352 + ET * N]
        alog_st = xbuf[:, 4608:4608 + ET * N]

        for src, dst in [(wd1t, wd1_st), (wd2t, wd2_st), (wbt, wb_st),
                         (wct, wc_st), (alog, alog_st)]:
            p.emit(SY, lambda e, s=src, d=dst: e.dma_start(out=d, in_=s),
                   inc=("sqp", 16))
        for src, dst in [(normw, normw_c[:]), (convw, convw_c[:]),
                         (convb, convb_c[:]), (wd2b, wd2b_c[:]),
                         (wdvec, wdvec_c[:]), (fbias, fbias_c[:]),
                         (fws, fws_c[:]), (ident_c, identf[:]),
                         (onesb_c, onesb[:]), (onesf_c, onesf[:]),
                         (selb_c, selb[:])]:
            p.emit(SY, lambda e, s=src, d=dst: e.dma_start(out=d, in_=s),
                   inc=("sqp", 16))
        sq_small = p.cur("sqp")

        p.wait(V, "sqp", sq_small)
        p.emit(V, lambda e: e.tensor_copy(out=wd1_bf[:], in_=wd1_st))
        p.emit(V, lambda e: e.tensor_copy(out=wd2_bf[:], in_=wd2_st))
        p.emit(V, lambda e: e.tensor_copy(out=wb_bf[:], in_=wb_st))
        p.emit(V, lambda e: e.tensor_copy(out=wc_bf[:], in_=wc_st))
        p.emit(V, lambda e: e.tensor_copy(out=identb[:], in_=identf[:]))
        sv_cp = v_inc()
        p.wait(S, "sv", sv_cp)
        p.emit(S, lambda e: e.activation(out=acol[:], in_=alog_st, func=AF.Exp))
        ss_acol = s_inc()
        p.wait(V, "ss", ss_acol)
        p.emit(V, lambda e: e.tensor_scalar(out=acol[:], in0=acol[:], scalar1=-1.0,
                                            scalar2=None, op0=OP.mult))
        for et in range(ET):
            for k in range(K):
                p.emit(V, lambda e, et=et, k=k: e.tensor_scalar(
                    out=cdiag[:, (et * K + k) * 128:(et * K + k + 1) * 128],
                    in0=identf[:], scalar1=convw_c[:, et * K + k:et * K + k + 1],
                    scalar2=None, op0=OP.mult))
            p.emit(V, lambda e, et=et: e.tensor_scalar(
                out=wdiag[:, et * 128:(et + 1) * 128],
                in0=identf[:], scalar1=wdvec_c[:, et:et + 1],
                scalar2=None, op0=OP.mult))
        p.emit(V, lambda e: e.memset(epsc[:], LN_EPS))
        p.emit(V, lambda e: e.memset(hstate[:], 0.0))
        p.emit(V, lambda e: e.memset(halo[:], 0.0))
        p.emit(V, lambda e: e.memset(statc[:], 0.0))
        sv_const = v_inc()

        last_fold_v = sv_const
        last_out_q = 0
        stg_f = xbuf[:, 0:E]

        def cache_piece(dma_in_fn, fold_fn, dma_out_fns):
            nonlocal last_fold_v, last_out_q
            p.wait(G, "sv", last_fold_v)
            gv = p.emit(G, dma_in_fn, inc=("gq", 16))
            p.wait(V, "gq", gv)
            p.wait(V, "sq", last_out_q)
            fold_fn()
            last_fold_v = v_inc()
            p.wait(SY, "sv", last_fold_v)
            for fn in dma_out_fns:
                p.emit(SY, fn, inc=("sq", 16))
            last_out_q = p.cur("sq")

        for dt in range(DTl):
            cache_piece(
                lambda e, dt=dt: e.dma_start(
                    out=stg_f, in_=w_in[dt * 128:(dt + 1) * 128, :]),
                lambda dt=dt: p.emit(V, lambda e, dt=dt: e.tensor_scalar(
                    out=xact[:, 0:E], in0=stg_f,
                    scalar1=normw_c[:, dt:dt + 1], scalar2=None, op0=OP.mult)),
                [lambda e, g=g, dt=dt: e.dma_start(
                    out=wc_in[g, dt], in_=xact[:, g * 512:(g + 1) * 512])
                 for g in range(4)])
        for et in range(ET):
            def fold_skip(et=et):
                for dt in range(DTl):
                    p.emit(V, lambda e, dt=dt: e.tensor_scalar(
                        out=xact[:, dt * 128:(dt + 1) * 128],
                        in0=xbuf[:, dt * 128:(dt + 1) * 128],
                        scalar1=normw_c[:, dt:dt + 1], scalar2=None, op0=OP.mult))
            cache_piece(
                lambda e, et=et: e.dma_start(
                    out=xbuf[:, 0:DTl * 128].rearrange("p (dt e) -> p dt e", e=128),
                    in_=w_skip[et].rearrange("(dt p) e -> p dt e", p=128)),
                fold_skip,
                [lambda e, et=et: e.dma_start(
                    out=wc_skip[et].rearrange("dt p e -> p dt e"),
                    in_=xact[:, 0:DTl * 128].rearrange("p (dt e) -> p dt e", e=128))])
        for et in range(ET):
            cache_piece(
                lambda e, et=et: e.dma_start(out=xbuf[:, 0:D], in_=w_out[et]),
                lambda: p.emit(V, lambda e: e.tensor_copy(
                    out=xact[:, 0:D], in_=xbuf[:, 0:D])),
                [lambda e, et=et: e.dma_start(out=wc_out[et], in_=xact[:, 0:D])])
        for dt in range(DTl):
            cache_piece(
                lambda e, dt=dt: e.dma_start(
                    out=xbuf[:, 0:D // 4].bitcast(I8),
                    in_=w_q[dt * 128:(dt + 1) * 128, :]),
                lambda: p.emit(V, lambda e: e.tensor_copy(
                    out=xact[:, 0:D], in_=xbuf[:, 0:D // 4].bitcast(I8))),
                [lambda e, hf=hf, dt=dt: e.dma_start(
                    out=wc_wq[hf, dt], in_=xact[:, hf * 512:(hf + 1) * 512])
                 for hf in range(2)])
        sq_cache = p.cur("sq")

        p.wait(SY, "sq", sq_cache)
        for tt in range(TT):
            p.emit(SY, lambda e, tt=tt: e.dma_start(
                out=xba(0, tt), in_=x_in[tt * 128:(tt + 1) * 128, :]),
                inc=("sx0", 16))
        xin_done = [("sx0", p.cur("sx0"))]

        winq_arr = [0, 0]
        wsk_arr = [0] * 4
        wo_arr = [0] * 4
        wq_arr = [0] * 8
        st_trT = 0
        svW_l = {}
        sgW_l = {}
        pend_hcopy = None
        ssbce_h = [0, 0]
        styacc_l = {}
        ssbce_l = {}
        sv_y2_prev_last = 0

        for c in range(NCH_RUN):
            par = c % 2

            if c + 1 < NCH_RUN:
                sxn = "sx%d" % ((c + 1) % 2)
                p.wait(SY, "sox", p.cur("sox"))
                for tt in range(TT):
                    p.emit(SY, lambda e, tt=tt, c=c: e.dma_start(
                        out=xba((c + 1) % 2, tt),
                        in_=x_in[(c + 1) * CH + tt * 128:(c + 1) * CH + (tt + 1) * 128, :]),
                        inc=(sxn, 16))
                xin_done.append((sxn, p.cur(sxn)))

            winq_vals = [None] * 4
            wsk_vals = [None] * ET
            wo_vals = [None] * (2 * ET)
            wqv = [None] * (2 * DTl)

            def load_win(g):
                buf = g % 2
                p.wait(SY, "st", winq_arr[buf])
                v = p.emit(SY, lambda e, g=g, buf=buf: e.dma_start(
                    out=win_q[:, buf * DTl * 512:(buf + 1) * DTl * 512]
                        .rearrange("p (dt e) -> p dt e", e=512),
                    in_=wc_in[g].rearrange("dt p e -> p dt e")),
                    inc=("win%d" % buf, 16))
                winq_vals[g] = ("win%d" % buf, v)

            def load_wsk(et):
                buf = et % 4
                p.wait(SY, "st", wsk_arr[buf])
                v = p.emit(SY, lambda e, et=et, buf=buf: e.dma_start(
                    out=wsk_t[:, buf * DTl * 128:(buf + 1) * DTl * 128]
                        .rearrange("p (dt e) -> p dt e", e=128),
                    in_=wc_skip[et].rearrange("dt p e -> p dt e")),
                    inc=("wsk%d" % buf, 16))
                wsk_vals[et] = ("wsk%d" % buf, v)

            def load_wo(i):
                if NOWO:
                    wo_vals[i] = ("wo0", 0)
                    return
                et = i % ET
                buf = i % 4
                p.wait(SY, "st", wo_arr[buf])
                v = p.emit(SY, lambda e, et=et, buf=buf: e.dma_start(
                    out=wo_t[:, buf * D:(buf + 1) * D], in_=wc_out[et]),
                    inc=("wo%d" % buf, 16))
                wo_vals[i] = ("wo%d" % buf, v)

            def load_wq(i):
                hf, dt = divmod(i, DTl)
                buf = i % 8
                p.wait(SY, "st", wq_arr[buf])
                v = p.emit(SY, lambda e, hf=hf, dt=dt, buf=buf: e.dma_start(
                    out=wq_t[:, buf * 512:(buf + 1) * 512], in_=wc_wq[hf, dt]),
                    inc=("wq%d" % buf, 16))
                wqv[i] = ("wq%d" % buf, v)

            load_win(0)
            load_win(1)
            for et in range(4):
                load_wsk(et)
            for i in range(4):
                load_wo(i)
            for i in range(8):
                load_wq(i)

            if PH >= 1:
                p.wait(V, xin_done[c][0], xin_done[c][1])
                tr_evac_s = []
                p.wait(V, "st", st_trT)
                for tt in range(TT):
                    xp = xba(par, tt)
                    sqh = sqtmp[:, 0:D] if tt % 2 == 0 else xnsf[:, 0:D]
                    p.emit(V, lambda e, xp=xp, sqh=sqh: e.tensor_tensor(
                        out=sqh, in0=xp, in1=xp, op=OP.mult))
                    p.emit(V, lambda e, xp=xp, tt=tt: e.tensor_reduce(
                        out=sc(0, tt), in_=xp, axis=mybir.AxisListType.X, op=OP.add))
                    p.emit(V, lambda e, tt=tt, sqh=sqh: e.tensor_reduce(
                        out=sc(1, tt), in_=sqh,
                        axis=mybir.AxisListType.X, op=OP.add))
                sv_st = v_inc()
                p.wait(V, "sv", sv_st)
                for tt in range(TT):
                    p.emit(V, lambda e, tt=tt: e.tensor_scalar(
                        out=sc(2, tt), in0=sc(0, tt), scalar1=1.0 / D, scalar2=None,
                        op0=OP.mult))
                sv_st = v_inc()
                p.wait(V, "sv", sv_st)
                for tt in range(TT):
                    p.emit(V, lambda e, tt=tt: e.tensor_tensor(
                        out=sc(12, tt), in0=sc(2, tt), in1=sc(2, tt), op=OP.mult))
                    p.emit(V, lambda e, tt=tt: e.tensor_scalar(
                        out=sc(3, tt), in0=sc(1, tt), scalar1=1.0 / D, scalar2=None,
                        op0=OP.mult))
                sv_st = v_inc()
                p.wait(V, "sv", sv_st)
                for tt in range(TT):
                    p.emit(V, lambda e, tt=tt: e.tensor_tensor(
                        out=sc(3, tt), in0=sc(3, tt), in1=sc(12, tt), op=OP.subtract))
                sv_st = v_inc()
                p.wait(S, "sv", sv_st)
                for tt in range(TT):
                    p.emit(S, lambda e, tt=tt: e.activation(
                        out=sc(12, tt), in_=sc(3, tt), func=AF.Ln, bias=epsc[:, 0:1]))
                for tt in range(TT):
                    p.emit(S, lambda e, tt=tt: e.activation(
                        out=sc(4, tt), in_=sc(12, tt), func=AF.Exp, scale=-0.5))
                ss_st = s_inc()
                p.wait(V, "ss", ss_st)
                for tt in range(TT):
                    p.emit(V, lambda e, tt=tt: e.tensor_tensor(
                        out=sc(13, tt), in0=sc(4, tt), in1=sc(4, tt), op=OP.mult))
                sv_st = v_inc()
                p.wait(V, "sv", sv_st)
                for tt in range(TT):
                    p.emit(V, lambda e, tt=tt: e.tensor_tensor(
                        out=sc(13, tt), in0=sc(13, tt), in1=sc(3, tt), op=OP.mult))
                sv_st = v_inc()
                p.wait(S, "sv", sv_st)
                for tt in range(TT):
                    p.emit(S, lambda e, tt=tt: e.activation(
                        out=sc(14, tt), in_=sc(13, tt), func=AF.Ln, bias=epsc[:, 0:1]))
                for tt in range(TT):
                    p.emit(S, lambda e, tt=tt: e.activation(
                        out=sc(5, tt), in_=sc(14, tt), func=AF.Exp, scale=-0.5))
                ss_st = s_inc()
                p.wait(V, "ss", ss_st)
                for tt in range(TT):
                    p.emit(V, lambda e, tt=tt: e.tensor_tensor(
                        out=sc(6, tt), in0=sc(4, tt), in1=sc(5, tt), op=OP.mult))
                    p.emit(V, lambda e, tt=tt: e.tensor_scalar(
                        out=sc(11, tt), in0=sc(4, tt), scalar1=1.0, scalar2=None,
                        op0=OP.add))
                    p.emit(V, lambda e, xp=xba(par, tt), tt=tt: e.tensor_scalar(
                        out=xp, in0=xp, scalar1=sc(2, tt), scalar2=None,
                        op0=OP.subtract))
                v_inc()
                for tt in range(TT):
                    p.emit(V, lambda e, xp=xba(par, tt), tt=tt: e.tensor_reduce(
                        out=sc(7, tt), in_=xp, axis=mybir.AxisListType.X, op=OP.max,
                        apply_absolute_value=True))
                sv_st = v_inc()
                p.wait(V, "sv", sv_st)
                for tt in range(TT):
                    p.emit(V, lambda e, tt=tt: e.tensor_tensor(
                        out=sc(8, tt), in0=sc(7, tt), in1=sc(4, tt), op=OP.mult))
                sv_st = v_inc()
                p.wait(V, "sv", sv_st)
                for tt in range(TT):
                    p.emit(V, lambda e, tt=tt: e.tensor_scalar(
                        out=sc(8, tt), in0=sc(8, tt), scalar1=Q_EPS, scalar2=1.0 / 127.0,
                        op0=OP.max, op1=OP.mult))
                sv_st = v_inc()
                p.wait(V, "sv", sv_st)
                for tt in range(TT):
                    p.emit(V, lambda e, tt=tt: e.reciprocal(out=sc(9, tt), in_=sc(8, tt)))
                sv_st = v_inc()
                p.wait(V, "sv", sv_st)
                for tt in range(TT):
                    p.emit(V, lambda e, tt=tt: e.tensor_tensor(
                        out=sc(10, tt), in0=sc(4, tt), in1=sc(9, tt), op=OP.mult))
                sv_st = v_inc()
                p.wait(V, "sv", sv_st)
                st_tr_q = 0
                for tt in range(TT):
                    xp = xba(par, tt)
                    p.wait(V, "st", st_tr_q)
                    p.emit(V, lambda e, xp=xp, tt=tt: e.tensor_scalar(
                        out=sqtmp[:, 0:D], in0=xp, scalar1=sc(10, tt), scalar2=MAGIC,
                        op0=OP.mult, op1=OP.add))
                    p.emit(V, lambda e: e.tensor_scalar(
                        out=sqtmp[:, 0:D], in0=sqtmp[:, 0:D], scalar1=MAGIC, scalar2=None,
                        op0=OP.subtract))
                    p.emit(V, lambda e, xp=xp, tt=tt: e.tensor_scalar(
                        out=xnsf[:], in0=xp, scalar1=sc(6, tt), scalar2=None,
                        op0=OP.mult))
                    vv = v_inc()
                    p.wait(T, "sv", vv)
                    for j in range(2 * DTl):
                        src = sqtmp if j < DTl else xnsf
                        dt = j % DTl
                        dst = xqT if j < DTl else xnsT
                        slot = j % 4
                        if len(tr_evac_s) >= 4:
                            p.wait(T, "ss", tr_evac_s[-4])
                        stv = p.emit(T, lambda e, src=src, dt=dt, slot=slot: e.transpose(
                            pb[slot][:, 0:128], src[:, dt * 128:(dt + 1) * 128],
                            identf[:]), inc=("st", 1))
                        p.wait(S, "st", stv)
                        p.emit(S, lambda e, dst=dst, dt=dt, tt=tt, slot=slot: e.copy(
                            out=dst[:, dt * 512 + tt * 128:dt * 512 + (tt + 1) * 128],
                            in_=pb[slot][:, 0:128]))
                        tr_evac_s.append(s_inc())
                    st_tr_q = p.cur("st")
                    st_trT = p.cur("st")
                ss_tphase = p.cur("ss")
    
                if c == 0:
                    dump("xq5a", xqT[:, 5 * 512:6 * 512])
                    dump("xq0", xqT[:, 0:512])
                    dump("xns0", xnsT[:, 0:512])
                    dump("stat0", statc[:])
    
            if PH >= 2:
                ss_grp_evac = [0] * 4
                for g in range(4):
                    buf = g % 2
                    p.wait(T, winq_vals[g][0], winq_vals[g][1])
                    p.wait(T, "ss", ss_tphase)
                    if g >= 1:
                        p.wait(T, "ss", ss_grp_evac[g - 1])
                    for dt in range(DTl):
                        for ei in range(4):
                            p.emit(T, lambda e, dt=dt, ei=ei, buf=buf: e.matmul(
                                pb[ei][:],
                                win_q[:, buf * DTl * 512 + dt * 512 + ei * 128:
                                      buf * DTl * 512 + dt * 512 + (ei + 1) * 128],
                                xnsT[:, dt * 512:(dt + 1) * 512],
                                start=(dt == 0), stop=(dt == DTl - 1)),
                                inc=("st", 1) if (dt == DTl - 1 and ei == 3) else None)
                    winq_arr[buf] = p.cur("st")
                    if g + 2 < 4:
                        load_win(g + 2)
                    st_inproj = p.cur("st")
                    for ei in range(4):
                        et = g * 4 + ei
                        ep = et % 2
                        p.wait(S, "st", st_inproj)
                        p.emit(S, lambda e, et=et, ep=ep: e.copy(
                            out=xtl[:, ep * 515 + 0:ep * 515 + 3],
                            in_=halo[:, et * 3:(et + 1) * 3]))
                        p.emit(S, lambda e, ei=ei, ep=ep: e.copy(
                            out=xtl[:, ep * 515 + 3:ep * 515 + 515], in_=pb[ei][:]))
                        ssv = s_inc()
                        if ei == 3:
                            ss_grp_evac[g] = ssv
                        p.wait(T, "ss", ssv)
                        for k in range(K):
                            p.emit(T, lambda e, et=et, k=k, ep=ep: e.matmul(
                                pb[4 + ep][:],
                                cdiag[:, (et * K + k) * 128:(et * K + k + 1) * 128],
                                xtl[:, ep * 515 + k:ep * 515 + k + 512],
                                start=(k == 0), stop=(k == K - 1)),
                                inc=("st", 1) if k == K - 1 else None)
                        stv = p.cur("st")
                        p.wait(S, "st", stv)
                        p.emit(S, lambda e, et=et, ep=ep: e.activation(
                            out=xact[:, et * 512:(et + 1) * 512],
                            in_=pb[4 + ep][:], func=AF.Silu,
                            bias=convb_c[:, et:et + 1]))
                        p.emit(S, lambda e, et=et, ep=ep: e.copy(
                            out=halo[:, et * 3:(et + 1) * 3],
                            in_=xtl[:, ep * 515 + 512:ep * 515 + 515]))
                        s_inc()
                        p.wait(T, wsk_vals[et][0], wsk_vals[et][1])
                        for dt in range(DTl):
                            p.emit(T, lambda e, et=et, dt=dt, ep=ep: e.matmul(
                                pb[6 + ep][:],
                                wsk_t[:, (et % 4) * DTl * 128 + dt * 128:
                                      (et % 4) * DTl * 128 + (dt + 1) * 128],
                                xnsT[:, dt * 512:(dt + 1) * 512],
                                start=(dt == 0), stop=(dt == DTl - 1)),
                                inc=("st", 1) if dt == DTl - 1 else None)
                        wsk_arr[et % 4] = p.cur("st")
                        if et + 4 < ET:
                            load_wsk(et + 4)
                        stv = p.cur("st")
                        p.wait(S, "st", stv)
                        p.emit(S, lambda e, et=et, ep=ep: e.activation(
                            out=ss_bf[:, et * 512:(et + 1) * 512],
                            in_=pb[6 + ep][:], func=AF.Silu))
                        s_inc()
                ss_pphase = p.cur("ss")
    
                if c == 0:
                    dump("xact0", xact[:, 0:512])
                    dump("ssbf0", ss_bf[:, 0:512])
    
            if PH >= 3:
                p.wait(T, "ss", ss_pphase)
                for et in range(ET):
                    p.emit(T, lambda e, et=et: e.matmul(
                        pb[0][0:DT, :], wd1_bf[:, et * DT:(et + 1) * DT],
                        xact[:, et * 512:(et + 1) * 512],
                        start=(et == 0), stop=(et == ET - 1)),
                        inc=("st", 1) if et == ET - 1 else None)
                for et in range(ET):
                    p.emit(T, lambda e, et=et: e.matmul(
                        pb[1][0:N, :], wb_bf[:, et * N:(et + 1) * N],
                        xact[:, et * 512:(et + 1) * 512],
                        start=(et == 0), stop=(et == ET - 1)),
                        inc=("st", 1) if et == ET - 1 else None)
                for et in range(ET):
                    p.emit(T, lambda e, et=et: e.matmul(
                        pb[2][0:N, :], wc_bf[:, et * N:(et + 1) * N],
                        xact[:, et * 512:(et + 1) * 512],
                        start=(et == 0), stop=(et == ET - 1)),
                        inc=("st", 1) if et == ET - 1 else None)
                st_small = p.cur("st")
                p.wait(V, "st", st_small)
                p.emit(V, lambda e: e.tensor_copy(out=z1bf[:], in_=pb[0][0:DT, :]))
                p.emit(V, lambda e: e.tensor_copy(out=btsb[:], in_=pb[1][0:N, :]))
                p.emit(V, lambda e: e.tensor_copy(out=ctsb[:], in_=pb[2][0:N, :]))
                p.emit(V, lambda e: e.tensor_copy(
                    out=sqtmp[:, 0:TT],
                    in_=statc[:].rearrange("p (t c) -> p t c", c=16)[:, :, 8]))
                sv_z1 = v_inc()
                p.wait(T, "sv", sv_z1)
                for tt in range(TT):
                    p.emit(T, lambda e, tt=tt: e.transpose(
                        pb[3][0:1, tt * 128:(tt + 1) * 128], sqtmp[:, tt:tt + 1],
                        identf[:]), inc=("st", 1) if tt == TT - 1 else None)
                stv = p.cur("st")
                p.wait(S, "st", stv)
                p.emit(S, lambda e: e.copy(out=xsr[:], in_=pb[3][0:1, 0:TT * 128]))
                ss_xsr = s_inc()
                p.wait(T, "ss", ss_xsr)
                for tt in range(TT):
                    p.emit(T, lambda e, tt=tt: e.matmul(
                        pb[3][:, tt * 128:(tt + 1) * 128], onesf[:],
                        xsr[0:1, tt * 128:(tt + 1) * 128], start=True, stop=True),
                        inc=("st", 1) if tt == TT - 1 else None)
                stv = p.cur("st")
                p.wait(S, "st", stv)
                p.emit(S, lambda e: e.copy(out=xsrep[:], in_=pb[3][:]))
                ss_xsrep = s_inc()
    
                if c == 0:
                    dump("z1", z1bf[:])
                    dump("bt", btsb[:])
                    dump("xsrep", xsrep[:])
    
            if PH >= 4:
                sv_y2 = [0] * ET
                for blk in range(4):
                    ets = [blk * 4 + i for i in range(4)]
                    ss_sp_of = {}
                    for ei, et in enumerate(ets):
                        wp = 6 + (ei % 2)
                        p.wait(T, "sv", sv_z1)
                        if blk > 0:
                            p.wait(T, "sv", sv_y2[(blk - 1) * 4 + ei])
                        if ei >= 2:
                            p.wait(T, "ss", ss_sp_of[ei - 2])
                        p.wait(T, "ss", ssbce_h[1])
                        p.emit(T, lambda e, et=et, wp=wp: e.matmul(
                            pb[wp][:], wd2_bf[:, et * 128:(et + 1) * 128], z1bf[:],
                            start=True, stop=True), inc=("st", 1))
                        st_dpre = p.cur("st")
                        p.wait(S, "st", st_dpre)
                        p.emit(S, lambda e, et=et, ei=ei, wp=wp: e.activation(
                            out=t1f[:, (ei % 2) * 512:(ei % 2) * 512 + 512], in_=pb[wp][:],
                            func=AF.Exp, bias=wd2b_c[:, et:et + 1]))
                        ss_sp = s_inc()
                        ss_sp_of[ei] = ss_sp
                        p.wait(V, "ss", ss_sp)
                        p.emit(V, lambda e, ei=ei: e.tensor_scalar(
                            out=t1f[:, (ei % 2) * 512:(ei % 2) * 512 + 512],
                            in0=t1f[:, (ei % 2) * 512:(ei % 2) * 512 + 512],
                            scalar1=1.0, scalar2=None, op0=OP.add))
                        sv_sp = v_inc()
                        p.wait(S, "sv", sv_sp)
                        p.emit(S, lambda e, ei=ei: e.activation(
                            out=dl[:, ei * 512:(ei + 1) * 512],
                            in_=t1f[:, (ei % 2) * 512:(ei % 2) * 512 + 512], func=AF.Ln))
                        ss_dl = s_inc()
                        p.wait(V, "ss", ss_dl)
                        p.emit(V, lambda e, et=et, ei=ei: e.tensor_tensor(
                            out=dx[:, ei * 512:(ei + 1) * 512],
                            in0=dl[:, ei * 512:(ei + 1) * 512],
                            in1=xact[:, et * 512:(et + 1) * 512], op=OP.mult))
                        v_inc()
                        p.emit(T, lambda e, et=et, ei=ei: e.matmul(
                            pb[ei][:], wdiag[:, et * 128:(et + 1) * 128],
                            xact[:, et * 512:(et + 1) * 512], start=True, stop=False,
                            skip_group_check=True))
                    viter = 0
                    def emit_bcast(n, blk=blk):
                        bp = 4 + 2 * (n % 2)
                        p.wait(T, "ss", ssbce_h[n % 2])
                        if n % 2 == 1:
                            p.wait(T, "ss", ss_sp_of[3])
                        p.emit(T, lambda e, n=n, bp=bp: e.matmul(
                            pb[bp][:], selb[:, n * 128:(n + 1) * 128], btsb[:],
                            start=True, stop=True))
                        return p.emit(T, lambda e, n=n, bp=bp: e.matmul(
                            pb[bp + 1][:], selb[:, n * 128:(n + 1) * 128], ctsb[:],
                            start=True, stop=True), inc=("st", 1))
                    st_bc_h = {0: emit_bcast(0), 1: emit_bcast(1)}
                    for n in range(N):
                        nb = n % 2
                        bp = 4 + 2 * nb
                        p.wait(S, "st", st_bc_h[n])
                        if n >= 2:
                            p.wait(S, "sv", svW_l[(c, blk, n - 2, 3)])
                        p.emit(S, lambda e, nb=nb, bp=bp: e.copy(
                            out=brep[:, nb * 512:(nb + 1) * 512], in_=pb[bp][:]))
                        p.emit(S, lambda e, nb=nb, bp=bp: e.copy(
                            out=crep[:, nb * 512:(nb + 1) * 512], in_=pb[bp + 1][:]))
                        ssbce_h[n % 2] = s_inc()
                        if n + 2 < N:
                            st_bc_h[n + 2] = emit_bcast(n + 2)
                        for ei, et in enumerate(ets):
                            sa = viter % 2
                            sw = viter % 4
                            if viter >= 2:
                                pv = viter - 2
                                p.wait(S, "sv", svW_l[(c, blk, (pv) // 4, pv % 4)])
                            if (c, blk, n - 2, ei) in sgW_l:
                                p.wait(S, "sg", sgW_l[(c, blk, n - 2, ei)])
                            p.emit(S, lambda e, et=et, ei=ei, n=n, sa=sa: e.activation(
                                out=abar[:, sa * 512:(sa + 1) * 512],
                                in_=dl[:, ei * 512:(ei + 1) * 512], func=AF.Exp,
                                scale=acol[:, et * N + n:et * N + n + 1]))
                            ss_n = s_inc()
                            p.wait(V, "ss", ss_n)
                            if viter >= 2:
                                pv = viter - 2
                                p.wait(V, "sg", sgW_l[(c, blk, pv // 4, pv % 4)])
                            p.emit(V, lambda e, ei=ei, nb=nb, sa=sa: e.tensor_tensor(
                                out=ubuf[:, sa * 512:(sa + 1) * 512],
                                in0=dx[:, ei * 512:(ei + 1) * 512],
                                in1=brep[:, nb * 512:(nb + 1) * 512], op=OP.mult))
                            p.emit(V, lambda e, et=et, n=n, sa=sa: e.tensor_tensor_scan(
                                out=ubuf[:, sa * 512:(sa + 1) * 512],
                                data0=abar[:, sa * 512:(sa + 1) * 512],
                                data1=ubuf[:, sa * 512:(sa + 1) * 512],
                                initial=hstate[:, et * N + n:et * N + n + 1],
                                op0=OP.mult, op1=OP.add))
                            svW_l[(c, blk, n, ei)] = v_inc()
                            if pend_hcopy is not None:
                                pet, pn, psa = pend_hcopy
                                p.emit(V, lambda e, pet=pet, pn=pn, psa=psa: e.tensor_copy(
                                    out=hstate[:, pet * N + pn:pet * N + pn + 1],
                                    in_=ubuf[:, psa * 512 + 511:psa * 512 + 512]))
                            pend_hcopy = (et, n, sa)
                            p.wait(G, "sv", svW_l[(c, blk, n, ei)])
                            if viter >= 4:
                                pv = viter - 4
                                p.wait(G, "st", styacc_l[(c, blk, pv // 4, pv % 4)])
                            p.emit(G, lambda e, nb=nb, sa=sa, sw=sw: e.tensor_tensor(
                                out=wbuf[:, sw * 512:(sw + 1) * 512],
                                in0=ubuf[:, sa * 512:(sa + 1) * 512],
                                in1=crep[:, nb * 512:(nb + 1) * 512], op=OP.mult),
                                inc=("sg", 1))
                            sgW_l[(c, blk, n, ei)] = p.cur("sg")
                            p.wait(T, "sg", sgW_l[(c, blk, n, ei)])
                            styacc_l[(c, blk, n, ei)] = p.emit(
                                T, lambda e, ei=ei, n=n, sw=sw: e.matmul(
                                    pb[ei][:], identb[:],
                                    wbuf[:, sw * 512:(sw + 1) * 512],
                                    start=False, stop=(n == N - 1),
                                    skip_group_check=True), inc=("st", 1))
                            viter += 1
                    for ei, et in enumerate(ets):
                        if ei == 0 and pend_hcopy is not None:
                            pass
                        p.wait(V, "st", styacc_l[(c, blk, N - 1, ei)])
                        p.emit(V, lambda e, et=et, ei=ei: e.tensor_tensor(
                            out=xact[:, et * 512:(et + 1) * 512], in0=pb[ei][:],
                            in1=ss_bf[:, et * 512:(et + 1) * 512], op=OP.mult))
                        sv_y2[et] = v_inc()
                        if pend_hcopy is not None:
                            pet, pn, psa = pend_hcopy
                            p.emit(V, lambda e, pet=pet, pn=pn, psa=psa: e.tensor_copy(
                                out=hstate[:, pet * N + pn:pet * N + pn + 1],
                                in_=ubuf[:, psa * 512 + 511:psa * 512 + 512]))
                            pend_hcopy = None

                if c == 0:
                    dump("dl0", dl[:, 0:512])
                    dump("y2_0", xact[:, 0:512])
                    dump("y2_15", xact[:, 15 * 512:16 * 512])
    
            if PH >= 5:
                p.wait(T, "ss", p.cur("ss"))
                p.wait(T, "sv", sv_y2[ET - 1])
                sv_mix_last = 0
                for hf in range(2 if PH5 >= 1 else 0):
                    for et in range(ET):
                        wo_i = hf * ET + et
                        buf = wo_i % 4
                        p.wait(T, wo_vals[wo_i][0], wo_vals[wo_i][1])
                        p.wait(T, "sv", sv_y2[et])
                        if hf == 1 and et == 0:
                            p.wait(T, "sv", sv_mix_last)
                        for di in range(4 if OF >= 1 else 0):
                            p.emit(T, lambda e, et=et, buf=buf, hf=hf, di=di: e.matmul(
                                pb[di][:],
                                wo_t[:, buf * D + hf * 512 + di * 128:
                                     buf * D + hf * 512 + (di + 1) * 128],
                                xact[:, et * 512:(et + 1) * 512],
                                start=(et == 0), stop=(et == ET - 1)),
                                inc=("st", 1) if di == 3 else None)
                        wo_arr[buf] = p.cur("st")
                        if wo_i + 4 < 2 * ET:
                            load_wo(wo_i + 4)
                    if c == 0 and hf == 0:
                        dump("xq5b", xqT[:, 5 * 512:6 * 512])
                        dump("wqt5", wq_t[:, 5 * 512:6 * 512])
                    p.wait(T, "sv", sv_y2[-1])
                    for dt in range(DTl if OF >= 2 else 0):
                        wq_i = hf * DTl + dt
                        buf = wq_i % 8
                        p.wait(T, wqv[wq_i][0], wqv[wq_i][1])
                        for oi in range(4):
                            p.emit(T, lambda e, dt=dt, buf=buf, oi=oi: e.matmul(
                                pb[4 + oi][:],
                                wq_t[:, buf * 512 + oi * 128:buf * 512 + (oi + 1) * 128],
                                xqT[:, dt * 512:(dt + 1) * 512],
                                start=(dt == 0), stop=(dt == DTl - 1)),
                                inc=("st", 1) if oi == 3 else None)
                        wq_arr[buf] = p.cur("st")
                        if wq_i + 8 < 2 * DTl:
                            load_wq(wq_i + 8)
                    st_half = p.cur("st")
                    p.wait(V, "st", st_half)
                    st_tr_pair = 0
                    for pr in range(2 if OF >= 3 else 0):
                        if pr == 1:
                            p.wait(V, "st", st_tr_pair)
                        for sl in range(2):
                            di = pr * 2 + sl
                            dt_g = hf * 4 + di
                            p.emit(V, lambda e, di=di, dt_g=dt_g, sl=sl: e.scalar_tensor_tensor(
                                out=mxs[:, sl * 512:(sl + 1) * 512], in0=pb[4 + di][:],
                                scalar=fws_c[:, dt_g:dt_g + 1], in1=xsrep[:],
                                op0=OP.mult, op1=OP.mult))
                            p.emit(V, lambda e, di=di, dt_g=dt_g, sl=sl: e.scalar_tensor_tensor(
                                out=mxs[:, sl * 512:(sl + 1) * 512],
                                in0=mxs[:, sl * 512:(sl + 1) * 512],
                                scalar=fbias_c[:, dt_g:dt_g + 1], in1=pb[di][:],
                                op0=OP.add, op1=OP.add))
                        if c == 0 and hf == 0 and pr == 0:
                            dump("ffnpart", mxs[:, 0:512], block=(V,))
                        sv_mx = v_inc()
                        if OF >= 4:
                            p.wait(T, "sv", sv_mx)
                            for sl in range(2):
                                di = pr * 2 + sl
                                for tt in range(TT):
                                    p.emit(T, lambda e, di=di, sl=sl, tt=tt: e.transpose(
                                        pb[di][:, tt * 128:(tt + 1) * 128],
                                        mxs[:, sl * 512 + tt * 128:sl * 512 + (tt + 1) * 128],
                                        identf[:]),
                                        inc=("st", 1) if (sl == 1 and tt == TT - 1) else None)
                            st_tr_pair = p.cur("st")
                        if OF >= 5:
                            p.wait(V, "st", st_tr_pair)
                            for sl in range(2):
                                di = pr * 2 + sl
                                dt_g = hf * 4 + di
                                for tt in range(TT):
                                    xp = xba(par, tt)
                                    p.emit(V, lambda e, xp=xp, tt=tt, dt_g=dt_g: e.tensor_scalar(
                                        out=xp[:, dt_g * 128:(dt_g + 1) * 128],
                                        in0=xp[:, dt_g * 128:(dt_g + 1) * 128],
                                        scalar1=sc(11, tt), scalar2=sc(2, tt),
                                        op0=OP.mult, op1=OP.add))
                            sv_fts = v_inc()
                            p.wait(V, "sv", sv_fts)
                            for sl in range(2):
                                di = pr * 2 + sl
                                dt_g = hf * 4 + di
                                for tt in range(TT):
                                    xp = xba(par, tt)
                                    p.emit(V, lambda e, xp=xp, di=di, tt=tt, dt_g=dt_g: e.tensor_tensor(
                                        out=xp[:, dt_g * 128:(dt_g + 1) * 128],
                                        in0=xp[:, dt_g * 128:(dt_g + 1) * 128],
                                        in1=pb[di][:, tt * 128:(tt + 1) * 128], op=OP.add))
                            sv_mix_last = v_inc()
                sv_final = p.cur("sv")
                p.wait(SY, "sv", sv_final)
                for tt in range(TT):
                    p.emit(SY, lambda e, tt=tt, c=c, par=par: e.dma_start(
                        out=y_out[c * CH + tt * 128:c * CH + (tt + 1) * 128, :],
                        in_=xba(par, tt)), inc=("sox", 16))

        p.wait(SY, "sox", p.cur("sox"))
        for s in ("sv", "ss", "st", "gq"):
            p.wait(SY, s, p.cur(s))
        for s in ("sv", "ss", "st", "gq"):
            p.wait(SY, s, p.cur(s))

        with nc.Block() as block:
            @block.sync
            def _(sync):
                for fn in p.ops["sync"]:
                    fn(sync)

            @block.gpsimd
            def _(gpsimd):
                for fn in p.ops["gpsimd"]:
                    fn(gpsimd)

            @block.vector
            def _(vector):
                for fn in p.ops["vector"]:
                    fn(vector)

            @block.scalar
            def _(scalar):
                for fn in p.ops["scalar"]:
                    fn(scalar)

            @block.tensor
            def _(tensor):
                for fn in p.ops["tensor"]:
                    fn(tensor)

    return nc


def host_prep(inputs):
    g = lambda k: np.ascontiguousarray(inputs[k])
    resid = g("L0_attn_resid")
    skip_w = g("L0_attn_skip_w")
    in_w = g("L0_attn_in_w")
    out_w = g("L0_attn_out_w")
    wq = g("L0_ffn_weight_q")

    def colmaj(v, ntile):
        return np.ascontiguousarray(v.reshape(ntile, 128).T)

    shared = {
        "w_in": np.ascontiguousarray(in_w.T),
        "w_skip": np.ascontiguousarray(skip_w.T.reshape(D, ET, 128).transpose(1, 0, 2)),
        "w_out": np.ascontiguousarray(out_w.T.reshape(ET, 128, D)),
        "w_q": np.ascontiguousarray(wq.T),
        "wd1t": np.ascontiguousarray(
            g("L0_attn_wd1").T.reshape(ET, 128, DT).transpose(1, 0, 2)
            .reshape(128, ET * DT)),
        "wd2t": np.ascontiguousarray(g("L0_attn_wd2").T),
        "wbt": np.ascontiguousarray(
            g("L0_attn_wb").T.reshape(ET, 128, N).transpose(1, 0, 2)
            .reshape(128, ET * N)),
        "wct": np.ascontiguousarray(
            g("L0_attn_wc").T.reshape(ET, 128, N).transpose(1, 0, 2)
            .reshape(128, ET * N)),
        "alog": np.ascontiguousarray(
            g("L0_attn_A_log").reshape(ET, 128, N).transpose(1, 0, 2)
            .reshape(128, ET * N)),
        "normw": colmaj(g("L0_attn_norm_w"), DTl),
        "convw": np.ascontiguousarray(
            g("L0_attn_conv_w").reshape(ET, 128, K).transpose(1, 0, 2)
            .reshape(128, ET * K)),
        "convb": colmaj(g("L0_attn_conv_b"), ET),
        "wd2b": colmaj(g("L0_attn_wd2_b"), ET),
        "wdvec": colmaj(g("L0_attn_W_D"), ET),
        "fbias": colmaj(g("L0_ffn_bias"), DTl),
        "fws": colmaj(g("L0_ffn_weight_scale"), DTl),
    }
    return [dict(shared, x=np.ascontiguousarray(resid[b])) for b in range(8)]


def kernel(**inputs):
    in_maps = host_prep(inputs)
    nc = build_core()
    res = run_bass_kernel_spmd(nc, in_maps, core_ids=list(range(8)))
    return np.stack([r["y"] for r in res.results], axis=0).astype(np.float32)


# version 07
# speedup vs baseline: 1.1825x; 1.1825x over previous
import numpy as np
import ml_dtypes
import concourse.bass as bass
import concourse.mybir as mybir
from concourse.bass_utils import run_bass_kernel_spmd

F32 = mybir.dt.float32
BF16 = mybir.dt.bfloat16
I8 = mybir.dt.int8
AF = mybir.ActivationFunctionType
OP = mybir.AluOpType

L, D, E, N, DT, K = 4096, 1024, 2048, 16, 64, 4
CH = 512
NCH = L // CH
ET, DTl = E // 128, D // 128
TT = CH // 128
MAGIC = float(np.float32(12582912.0))
LN_EPS = 1e-5
RMS_EPS = 1e-5
Q_EPS = 1e-6


def _act_set_idx(nc, *funcs):
    from concourse.hw_specs import get_activation_tables
    tabs = get_activation_tables(nc.m.arch)
    for idx, (name, fs) in enumerate(tabs.items()):
        if all(f in fs for f in funcs):
            return idx
    raise ValueError(f"no table set with {funcs}")


class Plan:
    ENG = ("sync", "gpsimd", "vector", "scalar", "tensor")

    def __init__(self, semnames):
        self.ops = {e: [] for e in self.ENG}
        self.cnt = {s: 0 for s in semnames}
        self.sems = None

    def emit(self, eng, fn, inc=None):
        if inc is not None:
            sem, amt = inc
            self.cnt[sem] += amt
            def wrapped(e, fn=fn, sem=sem, amt=amt):
                fn(e).then_inc(self.sems[sem], amt)
            self.ops[eng].append(wrapped)
            return self.cnt[sem]
        self.ops[eng].append(fn)
        return None

    def wait(self, eng, sem, val):
        if val <= 0:
            return
        assert val <= self.cnt[sem], (eng, sem, val, self.cnt[sem])
        self.ops[eng].append(lambda e, s=sem, v=val: e.wait_ge(self.sems[s], v))

    def cur(self, sem):
        return self.cnt[sem]


def build_core(dbg_tags=()):
    import os
    NCH_RUN = int(os.environ.get("MK_NCH", NCH))
    PH = int(os.environ.get("MK_PH", 9))
    OF = int(os.environ.get("MK_OF", 9))
    PH5 = int(os.environ.get("MK_PH5", 9))
    NOWO = os.environ.get("MK_NOWO", "0") == "1"
    NOOMM = os.environ.get("MK_NOOMM", "0") == "1"
    nc = bass.Bass("TRN2", target_bir_lowering=False,
                   detect_race_conditions=False)
    SET_EXPLN = _act_set_idx(nc, AF.Exp, AF.Ln)
    SET_SILU = _act_set_idx(nc, AF.Silu)

    x_in = nc.dram_tensor("x", [L, D], F32, kind="ExternalInput").ap()
    y_out = nc.dram_tensor("y", [L, D], F32, kind="ExternalOutput").ap()
    w_in = nc.dram_tensor("w_in", [D, E], F32, kind="ExternalInput").ap()
    w_skip = nc.dram_tensor("w_skip", [ET, D, 128], F32, kind="ExternalInput").ap()
    w_out = nc.dram_tensor("w_out", [ET, 128, D], F32, kind="ExternalInput").ap()
    w_q = nc.dram_tensor("w_q", [D, D], I8, kind="ExternalInput").ap()
    wd1t = nc.dram_tensor("wd1t", [128, ET * DT], F32, kind="ExternalInput").ap()
    wd2t = nc.dram_tensor("wd2t", [DT, E], F32, kind="ExternalInput").ap()
    wbt = nc.dram_tensor("wbt", [128, ET * N], F32, kind="ExternalInput").ap()
    wct = nc.dram_tensor("wct", [128, ET * N], F32, kind="ExternalInput").ap()
    alog = nc.dram_tensor("alog", [128, ET * N], F32, kind="ExternalInput").ap()
    normw = nc.dram_tensor("normw", [128, DTl], F32, kind="ExternalInput").ap()
    convw = nc.dram_tensor("convw", [128, ET * K], F32, kind="ExternalInput").ap()
    convb = nc.dram_tensor("convb", [128, ET], F32, kind="ExternalInput").ap()
    wd2b = nc.dram_tensor("wd2b", [128, ET], F32, kind="ExternalInput").ap()
    wdvec = nc.dram_tensor("wdvec", [128, ET], F32, kind="ExternalInput").ap()
    fbias = nc.dram_tensor("fbias", [128, DTl], F32, kind="ExternalInput").ap()
    fws = nc.dram_tensor("fws", [128, DTl], F32, kind="ExternalInput").ap()

    wc_in = nc.dram_tensor("wc_in", [4, DTl, 128, 512], BF16, kind="Internal").ap()
    wc_skip = nc.dram_tensor("wc_skip", [ET, DTl, 128, 128], BF16, kind="Internal").ap()
    wc_out = nc.dram_tensor("wc_out", [ET, 128, D], BF16, kind="Internal").ap()
    wc_wq = nc.dram_tensor("wc_wq", [2, DTl, 128, 512], BF16, kind="Internal").ap()

    ident_c = nc.inline_tensor(np.eye(128, dtype=np.float32), "identf_c").ap()
    onesb_c = nc.inline_tensor(np.ones((1, 128), dtype=ml_dtypes.bfloat16), "onesb_c").ap()
    _sel = np.zeros((16, 16 * 128), dtype=ml_dtypes.bfloat16)
    for _n in range(16):
        _sel[_n, _n * 128:(_n + 1) * 128] = 1.0
    selb_c = nc.inline_tensor(_sel, "selb_c").ap()
    onesf_c = nc.inline_tensor(np.ones((1, 128), dtype=np.float32), "onesf_c").ap()

    dbg = {}
    for tag, shape in dbg_tags:
        dbg[tag] = nc.dram_tensor("dbg_" + tag, list(shape), F32,
                                  kind="ExternalOutput").ap()

    from contextlib import ExitStack
    with ExitStack() as stack:
        sbt = lambda n, s, d: stack.enter_context(nc.sbuf_tensor(n, s, d))
        pst = lambda n: stack.enter_context(nc.psum_tensor(n, [128, 512], F32))

        wd1_bf = sbt("wd1_bf", [128, ET * DT], BF16)
        wd2_bf = sbt("wd2_bf", [DT, E], BF16)
        wb_bf = sbt("wb_bf", [128, ET * N], BF16)
        wc_bf = sbt("wc_bf", [128, ET * N], BF16)
        acol = sbt("acol", [128, ET * N], F32)
        normw_c = sbt("normw_c", [128, DTl], F32)
        convb_c = sbt("convb_c", [128, ET], F32)
        wd2b_c = sbt("wd2b_c", [128, ET], F32)
        fbias_c = sbt("fbias_c", [128, DTl], F32)
        fws_c = sbt("fws_c", [128, DTl], F32)
        convw_c = sbt("convw_c", [128, ET * K], F32)
        wdvec_c = sbt("wdvec_c", [128, ET], F32)
        identf = sbt("identf", [128, 128], F32)
        identb = sbt("identb", [128, 128], BF16)
        onesb = sbt("onesb", [1, 128], BF16)
        selb = sbt("selb", [16, 16 * 128], BF16)
        onesf = sbt("onesf", [1, 128], F32)
        cdiag = sbt("cdiag", [128, ET * K * 128], BF16)
        wdiag = sbt("wdiag", [128, ET * 128], BF16)
        hstate = sbt("hstate", [128, ET * N], F32)
        halo = sbt("halo", [128, ET * 3], BF16)
        epsc = sbt("epsc", [128, 1], F32)

        win_q = sbt("win_q", [128, 2 * DTl * 512], BF16)
        wsk_t = sbt("wsk_t", [128, 4 * DTl * 128], BF16)
        wo_t = sbt("wo_t", [128, 4 * D], BF16)
        wq_t = sbt("wq_t", [128, 8 * 512], BF16)

        xbuf = sbt("xbuf", [128, 2 * TT * D], F32)
        sqtmp = sbt("sqtmp", [128, D], F32)
        xnsf = sbt("xnsf", [128, D], F32)
        xnsT = sbt("xnsT", [128, DTl * 512], BF16)
        xqT = sbt("xqT", [128, DTl * 512], BF16)
        xact = sbt("xact", [128, ET * 512], BF16)
        ss_bf = sbt("ss_bf", [128, ET * 512], BF16)
        xtl = sbt("xtl", [128, 2 * 515], BF16)
        dl = sbt("dl", [128, 4 * 512], F32)
        dx = sbt("dx", [128, 4 * 512], BF16)
        t1f = sbt("t1f", [128, 2 * 512], F32)
        abar = sbt("abar", [128, 2 * 512], F32)
        ubuf = sbt("ubuf", [128, 2 * 512], BF16)
        wbuf = sbt("wbuf", [128, 4 * 512], BF16)
        brep = sbt("brep", [128, 2 * 512], BF16)
        crep = sbt("crep", [128, 2 * 512], BF16)
        btsb = sbt("btsb", [16, 512], BF16)
        ctsb = sbt("ctsb", [16, 512], BF16)
        z1bf = sbt("z1bf", [DT, 512], BF16)
        xsr = sbt("xsr", [1, TT * 128], F32)
        xsrep = sbt("xsrep", [128, 512], F32)
        mxs = sbt("mxs", [128, 2 * 512], F32)
        statc = sbt("statc", [128, TT * 16], F32)

        pb = [pst(f"pb{i}") for i in range(8)]

        semnames = (["sq", "gq", "sv", "ss", "st", "sqp", "sox", "sx0", "sx1"]
                    + [f"win{i}" for i in range(2)]
                    + [f"wsk{i}" for i in range(4)]
                    + [f"wo{i}" for i in range(4)]
                    + [f"wq{i}" for i in range(8)])
        p = Plan(semnames)
        sems = {}
        for sname in semnames:
            sems[sname] = stack.enter_context(nc.semaphore(sname))
        p.sems = sems

        V, S, T, SY, G = "vector", "scalar", "tensor", "sync", "gpsimd"

        def _attach(eng, sem):
            fn = p.ops[eng].pop()
            p.cnt[sem] += 1
            def wrapped(e, fn=fn, sem=sem):
                fn(e).then_inc(p.sems[sem], 1)
            p.ops[eng].append(wrapped)
            return p.cnt[sem]

        def v_inc():
            return _attach(V, "sv")

        def s_inc():
            return _attach(S, "ss")


        def sc(col, tt):
            return statc[:, tt * 16 + col:tt * 16 + col + 1]

        def xba(par, tt):
            return xbuf[:, (par * TT + tt) * D:(par * TT + tt + 1) * D]

        def dump(tag, ap, block=()):
            if tag not in dbg:
                return
            for s in ("sv", "ss", "st"):
                p.wait(G, s, p.cur(s))
            p.emit(G, lambda e, ap=ap, tag=tag: e.dma_start(
                out=dbg[tag], in_=ap), inc=("gq", 16))
            p.wait(G, "gq", p.cur("gq"))
            for eng in block:
                p.wait(eng, "gq", p.cur("gq"))

        wd1_st = xbuf[:, 0:ET * DT]
        wd2_st = xbuf[0:DT, 2048:2048 + E]
        wb_st = xbuf[:, 4096:4096 + ET * N]
        wc_st = xbuf[:, 4352:4352 + ET * N]
        alog_st = xbuf[:, 4608:4608 + ET * N]

        for src, dst in [(wd1t, wd1_st), (wd2t, wd2_st), (wbt, wb_st),
                         (wct, wc_st), (alog, alog_st)]:
            p.emit(SY, lambda e, s=src, d=dst: e.dma_start(out=d, in_=s),
                   inc=("sqp", 16))
        for src, dst in [(normw, normw_c[:]), (convw, convw_c[:]),
                         (convb, convb_c[:]), (wd2b, wd2b_c[:]),
                         (wdvec, wdvec_c[:]), (fbias, fbias_c[:]),
                         (fws, fws_c[:]), (ident_c, identf[:]),
                         (onesb_c, onesb[:]), (onesf_c, onesf[:]),
                         (selb_c, selb[:])]:
            p.emit(SY, lambda e, s=src, d=dst: e.dma_start(out=d, in_=s),
                   inc=("sqp", 16))
        sq_small = p.cur("sqp")

        p.wait(V, "sqp", sq_small)
        p.emit(V, lambda e: e.tensor_copy(out=wd1_bf[:], in_=wd1_st))
        p.emit(V, lambda e: e.tensor_copy(out=wd2_bf[:], in_=wd2_st))
        p.emit(V, lambda e: e.tensor_copy(out=wb_bf[:], in_=wb_st))
        p.emit(V, lambda e: e.tensor_copy(out=wc_bf[:], in_=wc_st))
        p.emit(V, lambda e: e.tensor_copy(out=identb[:], in_=identf[:]))
        sv_cp = v_inc()
        p.wait(S, "sv", sv_cp)
        p.emit(S, lambda e: e.activation(out=acol[:], in_=alog_st, func=AF.Exp))
        ss_acol = s_inc()
        p.wait(V, "ss", ss_acol)
        p.emit(V, lambda e: e.tensor_scalar(out=acol[:], in0=acol[:], scalar1=-1.0,
                                            scalar2=None, op0=OP.mult))
        for et in range(ET):
            for k in range(K):
                p.emit(V, lambda e, et=et, k=k: e.tensor_scalar(
                    out=cdiag[:, (et * K + k) * 128:(et * K + k + 1) * 128],
                    in0=identf[:], scalar1=convw_c[:, et * K + k:et * K + k + 1],
                    scalar2=None, op0=OP.mult))
            p.emit(V, lambda e, et=et: e.tensor_scalar(
                out=wdiag[:, et * 128:(et + 1) * 128],
                in0=identf[:], scalar1=wdvec_c[:, et:et + 1],
                scalar2=None, op0=OP.mult))
        p.emit(V, lambda e: e.memset(epsc[:], LN_EPS))
        p.emit(V, lambda e: e.memset(hstate[:], 0.0))
        p.emit(V, lambda e: e.memset(halo[:], 0.0))
        p.emit(V, lambda e: e.memset(statc[:], 0.0))
        sv_const = v_inc()

        last_fold_v = sv_const
        last_out_q = 0
        stg_f = xbuf[:, 0:E]

        def cache_piece(dma_in_fn, fold_fn, dma_out_fns):
            nonlocal last_fold_v, last_out_q
            p.wait(G, "sv", last_fold_v)
            gv = p.emit(G, dma_in_fn, inc=("gq", 16))
            p.wait(V, "gq", gv)
            p.wait(V, "sq", last_out_q)
            fold_fn()
            last_fold_v = v_inc()
            p.wait(SY, "sv", last_fold_v)
            for fn in dma_out_fns:
                p.emit(SY, fn, inc=("sq", 16))
            last_out_q = p.cur("sq")

        for dt in range(DTl):
            cache_piece(
                lambda e, dt=dt: e.dma_start(
                    out=stg_f, in_=w_in[dt * 128:(dt + 1) * 128, :]),
                lambda dt=dt: p.emit(V, lambda e, dt=dt: e.tensor_scalar(
                    out=xact[:, 0:E], in0=stg_f,
                    scalar1=normw_c[:, dt:dt + 1], scalar2=None, op0=OP.mult)),
                [lambda e, g=g, dt=dt: e.dma_start(
                    out=wc_in[g, dt], in_=xact[:, g * 512:(g + 1) * 512])
                 for g in range(4)])
        for et in range(ET):
            def fold_skip(et=et):
                for dt in range(DTl):
                    p.emit(V, lambda e, dt=dt: e.tensor_scalar(
                        out=xact[:, dt * 128:(dt + 1) * 128],
                        in0=xbuf[:, dt * 128:(dt + 1) * 128],
                        scalar1=normw_c[:, dt:dt + 1], scalar2=None, op0=OP.mult))
            cache_piece(
                lambda e, et=et: e.dma_start(
                    out=xbuf[:, 0:DTl * 128].rearrange("p (dt e) -> p dt e", e=128),
                    in_=w_skip[et].rearrange("(dt p) e -> p dt e", p=128)),
                fold_skip,
                [lambda e, et=et: e.dma_start(
                    out=wc_skip[et].rearrange("dt p e -> p dt e"),
                    in_=xact[:, 0:DTl * 128].rearrange("p (dt e) -> p dt e", e=128))])
        for et in range(ET):
            cache_piece(
                lambda e, et=et: e.dma_start(out=xbuf[:, 0:D], in_=w_out[et]),
                lambda: p.emit(V, lambda e: e.tensor_copy(
                    out=xact[:, 0:D], in_=xbuf[:, 0:D])),
                [lambda e, et=et: e.dma_start(out=wc_out[et], in_=xact[:, 0:D])])
        for dt in range(DTl):
            cache_piece(
                lambda e, dt=dt: e.dma_start(
                    out=xbuf[:, 0:D // 4].bitcast(I8),
                    in_=w_q[dt * 128:(dt + 1) * 128, :]),
                lambda: p.emit(V, lambda e: e.tensor_copy(
                    out=xact[:, 0:D], in_=xbuf[:, 0:D // 4].bitcast(I8))),
                [lambda e, hf=hf, dt=dt: e.dma_start(
                    out=wc_wq[hf, dt], in_=xact[:, hf * 512:(hf + 1) * 512])
                 for hf in range(2)])
        sq_cache = p.cur("sq")

        p.wait(SY, "sq", sq_cache)
        for tt in range(TT):
            p.emit(SY, lambda e, tt=tt: e.dma_start(
                out=xba(0, tt), in_=x_in[tt * 128:(tt + 1) * 128, :]),
                inc=("sx0", 16))
        xin_done = [("sx0", p.cur("sx0"))]

        winq_arr = [0, 0]
        wsk_arr = [0] * 4
        wo_arr = [0] * 4
        wq_arr = [0] * 8
        st_trT = 0
        svW_l = {}
        ssbce_h = [0, 0]
        styacc_l = {}
        ssbce_l = {}
        sv_y2_prev_last = 0

        for c in range(NCH_RUN):
            par = c % 2

            if c + 1 < NCH_RUN:
                sxn = "sx%d" % ((c + 1) % 2)
                p.wait(SY, "sox", p.cur("sox"))
                for tt in range(TT):
                    p.emit(SY, lambda e, tt=tt, c=c: e.dma_start(
                        out=xba((c + 1) % 2, tt),
                        in_=x_in[(c + 1) * CH + tt * 128:(c + 1) * CH + (tt + 1) * 128, :]),
                        inc=(sxn, 16))
                xin_done.append((sxn, p.cur(sxn)))

            winq_vals = [None] * 4
            wsk_vals = [None] * ET
            wo_vals = [None] * (2 * ET)
            wqv = [None] * (2 * DTl)

            def load_win(g):
                buf = g % 2
                p.wait(G, "st", winq_arr[buf])
                v = p.emit(G, lambda e, g=g, buf=buf: e.dma_start(
                    out=win_q[:, buf * DTl * 512:(buf + 1) * DTl * 512]
                        .rearrange("p (dt e) -> p dt e", e=512),
                    in_=wc_in[g].rearrange("dt p e -> p dt e")),
                    inc=("win%d" % buf, 16))
                winq_vals[g] = ("win%d" % buf, v)

            def load_wsk(et):
                buf = et % 4
                p.wait(G, "st", wsk_arr[buf])
                v = p.emit(G, lambda e, et=et, buf=buf: e.dma_start(
                    out=wsk_t[:, buf * DTl * 128:(buf + 1) * DTl * 128]
                        .rearrange("p (dt e) -> p dt e", e=128),
                    in_=wc_skip[et].rearrange("dt p e -> p dt e")),
                    inc=("wsk%d" % buf, 16))
                wsk_vals[et] = ("wsk%d" % buf, v)

            def load_wo(i):
                if NOWO:
                    wo_vals[i] = ("wo0", 0)
                    return
                et = i % ET
                buf = i % 4
                p.wait(G, "st", wo_arr[buf])
                v = p.emit(G, lambda e, et=et, buf=buf: e.dma_start(
                    out=wo_t[:, buf * D:(buf + 1) * D], in_=wc_out[et]),
                    inc=("wo%d" % buf, 16))
                wo_vals[i] = ("wo%d" % buf, v)

            def load_wq(i):
                hf, dt = divmod(i, DTl)
                buf = i % 8
                p.wait(G, "st", wq_arr[buf])
                v = p.emit(G, lambda e, hf=hf, dt=dt, buf=buf: e.dma_start(
                    out=wq_t[:, buf * 512:(buf + 1) * 512], in_=wc_wq[hf, dt]),
                    inc=("wq%d" % buf, 16))
                wqv[i] = ("wq%d" % buf, v)

            load_win(0)
            load_win(1)
            for et in range(4):
                load_wsk(et)
            for i in range(4):
                load_wo(i)
            for i in range(8):
                load_wq(i)

            if PH >= 1:
                p.wait(V, xin_done[c][0], xin_done[c][1])
                tr_evac_s = []
                p.wait(V, "st", st_trT)
                for tt in range(TT):
                    xp = xba(par, tt)
                    sqh = sqtmp[:, 0:D] if tt % 2 == 0 else xnsf[:, 0:D]
                    p.emit(V, lambda e, xp=xp, sqh=sqh: e.tensor_tensor(
                        out=sqh, in0=xp, in1=xp, op=OP.mult))
                    p.emit(V, lambda e, xp=xp, tt=tt: e.tensor_reduce(
                        out=sc(0, tt), in_=xp, axis=mybir.AxisListType.X, op=OP.add))
                    p.emit(V, lambda e, tt=tt, sqh=sqh: e.tensor_reduce(
                        out=sc(1, tt), in_=sqh,
                        axis=mybir.AxisListType.X, op=OP.add))
                sv_st = v_inc()
                p.wait(V, "sv", sv_st)
                for tt in range(TT):
                    p.emit(V, lambda e, tt=tt: e.tensor_scalar(
                        out=sc(2, tt), in0=sc(0, tt), scalar1=1.0 / D, scalar2=None,
                        op0=OP.mult))
                sv_st = v_inc()
                p.wait(V, "sv", sv_st)
                for tt in range(TT):
                    p.emit(V, lambda e, tt=tt: e.tensor_tensor(
                        out=sc(12, tt), in0=sc(2, tt), in1=sc(2, tt), op=OP.mult))
                    p.emit(V, lambda e, tt=tt: e.tensor_scalar(
                        out=sc(3, tt), in0=sc(1, tt), scalar1=1.0 / D, scalar2=None,
                        op0=OP.mult))
                sv_st = v_inc()
                p.wait(V, "sv", sv_st)
                for tt in range(TT):
                    p.emit(V, lambda e, tt=tt: e.tensor_tensor(
                        out=sc(3, tt), in0=sc(3, tt), in1=sc(12, tt), op=OP.subtract))
                sv_st = v_inc()
                p.wait(S, "sv", sv_st)
                for tt in range(TT):
                    p.emit(S, lambda e, tt=tt: e.activation(
                        out=sc(12, tt), in_=sc(3, tt), func=AF.Ln, bias=epsc[:, 0:1]))
                for tt in range(TT):
                    p.emit(S, lambda e, tt=tt: e.activation(
                        out=sc(4, tt), in_=sc(12, tt), func=AF.Exp, scale=-0.5))
                ss_st = s_inc()
                p.wait(V, "ss", ss_st)
                for tt in range(TT):
                    p.emit(V, lambda e, tt=tt: e.tensor_tensor(
                        out=sc(13, tt), in0=sc(4, tt), in1=sc(4, tt), op=OP.mult))
                sv_st = v_inc()
                p.wait(V, "sv", sv_st)
                for tt in range(TT):
                    p.emit(V, lambda e, tt=tt: e.tensor_tensor(
                        out=sc(13, tt), in0=sc(13, tt), in1=sc(3, tt), op=OP.mult))
                sv_st = v_inc()
                p.wait(S, "sv", sv_st)
                for tt in range(TT):
                    p.emit(S, lambda e, tt=tt: e.activation(
                        out=sc(14, tt), in_=sc(13, tt), func=AF.Ln, bias=epsc[:, 0:1]))
                for tt in range(TT):
                    p.emit(S, lambda e, tt=tt: e.activation(
                        out=sc(5, tt), in_=sc(14, tt), func=AF.Exp, scale=-0.5))
                ss_st = s_inc()
                p.wait(V, "ss", ss_st)
                for tt in range(TT):
                    p.emit(V, lambda e, tt=tt: e.tensor_tensor(
                        out=sc(6, tt), in0=sc(4, tt), in1=sc(5, tt), op=OP.mult))
                    p.emit(V, lambda e, tt=tt: e.tensor_scalar(
                        out=sc(11, tt), in0=sc(4, tt), scalar1=1.0, scalar2=None,
                        op0=OP.add))
                    p.emit(V, lambda e, xp=xba(par, tt), tt=tt: e.tensor_scalar(
                        out=xp, in0=xp, scalar1=sc(2, tt), scalar2=None,
                        op0=OP.subtract))
                v_inc()
                for tt in range(TT):
                    p.emit(V, lambda e, xp=xba(par, tt), tt=tt: e.tensor_reduce(
                        out=sc(7, tt), in_=xp, axis=mybir.AxisListType.X, op=OP.max,
                        apply_absolute_value=True))
                sv_st = v_inc()
                p.wait(V, "sv", sv_st)
                for tt in range(TT):
                    p.emit(V, lambda e, tt=tt: e.tensor_tensor(
                        out=sc(8, tt), in0=sc(7, tt), in1=sc(4, tt), op=OP.mult))
                sv_st = v_inc()
                p.wait(V, "sv", sv_st)
                for tt in range(TT):
                    p.emit(V, lambda e, tt=tt: e.tensor_scalar(
                        out=sc(8, tt), in0=sc(8, tt), scalar1=Q_EPS, scalar2=1.0 / 127.0,
                        op0=OP.max, op1=OP.mult))
                sv_st = v_inc()
                p.wait(V, "sv", sv_st)
                for tt in range(TT):
                    p.emit(V, lambda e, tt=tt: e.reciprocal(out=sc(9, tt), in_=sc(8, tt)))
                sv_st = v_inc()
                p.wait(V, "sv", sv_st)
                for tt in range(TT):
                    p.emit(V, lambda e, tt=tt: e.tensor_tensor(
                        out=sc(10, tt), in0=sc(4, tt), in1=sc(9, tt), op=OP.mult))
                sv_st = v_inc()
                p.wait(V, "sv", sv_st)
                st_tr_q = 0
                for tt in range(TT):
                    xp = xba(par, tt)
                    p.wait(V, "st", st_tr_q)
                    p.emit(V, lambda e, xp=xp, tt=tt: e.tensor_scalar(
                        out=sqtmp[:, 0:D], in0=xp, scalar1=sc(10, tt), scalar2=MAGIC,
                        op0=OP.mult, op1=OP.add))
                    p.emit(V, lambda e: e.tensor_scalar(
                        out=sqtmp[:, 0:D], in0=sqtmp[:, 0:D], scalar1=MAGIC, scalar2=None,
                        op0=OP.subtract))
                    p.emit(V, lambda e, xp=xp, tt=tt: e.tensor_scalar(
                        out=xnsf[:], in0=xp, scalar1=sc(6, tt), scalar2=None,
                        op0=OP.mult))
                    vv = v_inc()
                    p.wait(T, "sv", vv)
                    for j in range(2 * DTl):
                        src = sqtmp if j < DTl else xnsf
                        dt = j % DTl
                        dst = xqT if j < DTl else xnsT
                        slot = j % 4
                        if len(tr_evac_s) >= 4:
                            p.wait(T, "ss", tr_evac_s[-4])
                        stv = p.emit(T, lambda e, src=src, dt=dt, slot=slot: e.transpose(
                            pb[slot][:, 0:128], src[:, dt * 128:(dt + 1) * 128],
                            identf[:]), inc=("st", 1))
                        p.wait(S, "st", stv)
                        p.emit(S, lambda e, dst=dst, dt=dt, tt=tt, slot=slot: e.copy(
                            out=dst[:, dt * 512 + tt * 128:dt * 512 + (tt + 1) * 128],
                            in_=pb[slot][:, 0:128]))
                        tr_evac_s.append(s_inc())
                    st_tr_q = p.cur("st")
                    st_trT = p.cur("st")
                ss_tphase = p.cur("ss")
    
                if c == 0:
                    dump("xq5a", xqT[:, 5 * 512:6 * 512])
                    dump("xq0", xqT[:, 0:512])
                    dump("xns0", xnsT[:, 0:512])
                    dump("stat0", statc[:])
    
            if PH >= 2:
                ss_grp_evac = [0] * 4
                for g in range(4):
                    buf = g % 2
                    p.wait(T, winq_vals[g][0], winq_vals[g][1])
                    p.wait(T, "ss", ss_tphase)
                    if g >= 1:
                        p.wait(T, "ss", ss_grp_evac[g - 1])
                    for dt in range(DTl):
                        for ei in range(4):
                            p.emit(T, lambda e, dt=dt, ei=ei, buf=buf: e.matmul(
                                pb[ei][:],
                                win_q[:, buf * DTl * 512 + dt * 512 + ei * 128:
                                      buf * DTl * 512 + dt * 512 + (ei + 1) * 128],
                                xnsT[:, dt * 512:(dt + 1) * 512],
                                start=(dt == 0), stop=(dt == DTl - 1)),
                                inc=("st", 1) if (dt == DTl - 1 and ei == 3) else None)
                    winq_arr[buf] = p.cur("st")
                    if g + 2 < 4:
                        load_win(g + 2)
                    st_inproj = p.cur("st")
                    for ei in range(4):
                        et = g * 4 + ei
                        ep = et % 2
                        p.wait(S, "st", st_inproj)
                        p.emit(S, lambda e, et=et, ep=ep: e.copy(
                            out=xtl[:, ep * 515 + 0:ep * 515 + 3],
                            in_=halo[:, et * 3:(et + 1) * 3]))
                        p.emit(S, lambda e, ei=ei, ep=ep: e.copy(
                            out=xtl[:, ep * 515 + 3:ep * 515 + 515], in_=pb[ei][:]))
                        ssv = s_inc()
                        if ei == 3:
                            ss_grp_evac[g] = ssv
                        p.wait(T, "ss", ssv)
                        for k in range(K):
                            p.emit(T, lambda e, et=et, k=k, ep=ep: e.matmul(
                                pb[4 + ep][:],
                                cdiag[:, (et * K + k) * 128:(et * K + k + 1) * 128],
                                xtl[:, ep * 515 + k:ep * 515 + k + 512],
                                start=(k == 0), stop=(k == K - 1)),
                                inc=("st", 1) if k == K - 1 else None)
                        stv = p.cur("st")
                        p.wait(S, "st", stv)
                        p.emit(S, lambda e, et=et, ep=ep: e.activation(
                            out=xact[:, et * 512:(et + 1) * 512],
                            in_=pb[4 + ep][:], func=AF.Silu,
                            bias=convb_c[:, et:et + 1]))
                        p.emit(S, lambda e, et=et, ep=ep: e.copy(
                            out=halo[:, et * 3:(et + 1) * 3],
                            in_=xtl[:, ep * 515 + 512:ep * 515 + 515]))
                        s_inc()
                        p.wait(T, wsk_vals[et][0], wsk_vals[et][1])
                        for dt in range(DTl):
                            p.emit(T, lambda e, et=et, dt=dt, ep=ep: e.matmul(
                                pb[6 + ep][:],
                                wsk_t[:, (et % 4) * DTl * 128 + dt * 128:
                                      (et % 4) * DTl * 128 + (dt + 1) * 128],
                                xnsT[:, dt * 512:(dt + 1) * 512],
                                start=(dt == 0), stop=(dt == DTl - 1)),
                                inc=("st", 1) if dt == DTl - 1 else None)
                        wsk_arr[et % 4] = p.cur("st")
                        if et + 4 < ET:
                            load_wsk(et + 4)
                        stv = p.cur("st")
                        p.wait(S, "st", stv)
                        p.emit(S, lambda e, et=et, ep=ep: e.activation(
                            out=ss_bf[:, et * 512:(et + 1) * 512],
                            in_=pb[6 + ep][:], func=AF.Silu))
                        s_inc()
                ss_pphase = p.cur("ss")
    
                if c == 0:
                    dump("xact0", xact[:, 0:512])
                    dump("ssbf0", ss_bf[:, 0:512])
    
            if PH >= 3:
                p.wait(T, "ss", ss_pphase)
                for et in range(ET):
                    p.emit(T, lambda e, et=et: e.matmul(
                        pb[0][0:DT, :], wd1_bf[:, et * DT:(et + 1) * DT],
                        xact[:, et * 512:(et + 1) * 512],
                        start=(et == 0), stop=(et == ET - 1)),
                        inc=("st", 1) if et == ET - 1 else None)
                for et in range(ET):
                    p.emit(T, lambda e, et=et: e.matmul(
                        pb[1][0:N, :], wb_bf[:, et * N:(et + 1) * N],
                        xact[:, et * 512:(et + 1) * 512],
                        start=(et == 0), stop=(et == ET - 1)),
                        inc=("st", 1) if et == ET - 1 else None)
                for et in range(ET):
                    p.emit(T, lambda e, et=et: e.matmul(
                        pb[2][0:N, :], wc_bf[:, et * N:(et + 1) * N],
                        xact[:, et * 512:(et + 1) * 512],
                        start=(et == 0), stop=(et == ET - 1)),
                        inc=("st", 1) if et == ET - 1 else None)
                st_small = p.cur("st")
                p.wait(V, "st", st_small)
                p.emit(V, lambda e: e.tensor_copy(out=z1bf[:], in_=pb[0][0:DT, :]))
                p.emit(V, lambda e: e.tensor_copy(out=btsb[:], in_=pb[1][0:N, :]))
                p.emit(V, lambda e: e.tensor_copy(out=ctsb[:], in_=pb[2][0:N, :]))
                p.emit(V, lambda e: e.tensor_copy(
                    out=sqtmp[:, 0:TT],
                    in_=statc[:].rearrange("p (t c) -> p t c", c=16)[:, :, 8]))
                sv_z1 = v_inc()
                p.wait(T, "sv", sv_z1)
                for tt in range(TT):
                    p.emit(T, lambda e, tt=tt: e.transpose(
                        pb[3][0:1, tt * 128:(tt + 1) * 128], sqtmp[:, tt:tt + 1],
                        identf[:]), inc=("st", 1) if tt == TT - 1 else None)
                stv = p.cur("st")
                p.wait(S, "st", stv)
                p.emit(S, lambda e: e.copy(out=xsr[:], in_=pb[3][0:1, 0:TT * 128]))
                ss_xsr = s_inc()
                p.wait(T, "ss", ss_xsr)
                for tt in range(TT):
                    p.emit(T, lambda e, tt=tt: e.matmul(
                        pb[3][:, tt * 128:(tt + 1) * 128], onesf[:],
                        xsr[0:1, tt * 128:(tt + 1) * 128], start=True, stop=True),
                        inc=("st", 1) if tt == TT - 1 else None)
                stv = p.cur("st")
                p.wait(S, "st", stv)
                p.emit(S, lambda e: e.copy(out=xsrep[:], in_=pb[3][:]))
                ss_xsrep = s_inc()
    
                if c == 0:
                    dump("z1", z1bf[:])
                    dump("bt", btsb[:])
                    dump("xsrep", xsrep[:])
    
            if PH >= 4:
                sv_y2 = [0] * ET
                for blk in range(4):
                    ets = [blk * 4 + i for i in range(4)]
                    ss_sp_of = {}
                    for ei, et in enumerate(ets):
                        wp = 6 + (ei % 2)
                        p.wait(T, "sv", sv_z1)
                        if blk > 0:
                            p.wait(T, "sv", sv_y2[(blk - 1) * 4 + ei])
                        if ei >= 2:
                            p.wait(T, "ss", ss_sp_of[ei - 2])
                        p.wait(T, "ss", ssbce_h[1])
                        p.emit(T, lambda e, et=et, wp=wp: e.matmul(
                            pb[wp][:], wd2_bf[:, et * 128:(et + 1) * 128], z1bf[:],
                            start=True, stop=True), inc=("st", 1))
                        st_dpre = p.cur("st")
                        p.wait(S, "st", st_dpre)
                        p.emit(S, lambda e, et=et, ei=ei, wp=wp: e.activation(
                            out=t1f[:, (ei % 2) * 512:(ei % 2) * 512 + 512], in_=pb[wp][:],
                            func=AF.Exp, bias=wd2b_c[:, et:et + 1]))
                        ss_sp = s_inc()
                        ss_sp_of[ei] = ss_sp
                        p.wait(V, "ss", ss_sp)
                        p.emit(V, lambda e, ei=ei: e.tensor_scalar(
                            out=t1f[:, (ei % 2) * 512:(ei % 2) * 512 + 512],
                            in0=t1f[:, (ei % 2) * 512:(ei % 2) * 512 + 512],
                            scalar1=1.0, scalar2=None, op0=OP.add))
                        sv_sp = v_inc()
                        p.wait(S, "sv", sv_sp)
                        p.emit(S, lambda e, ei=ei: e.activation(
                            out=dl[:, ei * 512:(ei + 1) * 512],
                            in_=t1f[:, (ei % 2) * 512:(ei % 2) * 512 + 512], func=AF.Ln))
                        ss_dl = s_inc()
                        p.wait(V, "ss", ss_dl)
                        p.emit(V, lambda e, et=et, ei=ei: e.tensor_tensor(
                            out=dx[:, ei * 512:(ei + 1) * 512],
                            in0=dl[:, ei * 512:(ei + 1) * 512],
                            in1=xact[:, et * 512:(et + 1) * 512], op=OP.mult))
                        v_inc()
                        p.emit(T, lambda e, et=et, ei=ei: e.matmul(
                            pb[ei][:], wdiag[:, et * 128:(et + 1) * 128],
                            xact[:, et * 512:(et + 1) * 512], start=True, stop=False,
                            skip_group_check=True))
                    viter = 0
                    def emit_bcast(n, blk=blk):
                        bp = 4 + 2 * (n % 2)
                        p.wait(T, "ss", ssbce_h[n % 2])
                        if n % 2 == 1:
                            p.wait(T, "ss", ss_sp_of[3])
                        p.emit(T, lambda e, n=n, bp=bp: e.matmul(
                            pb[bp][:], selb[:, n * 128:(n + 1) * 128], btsb[:],
                            start=True, stop=True))
                        return p.emit(T, lambda e, n=n, bp=bp: e.matmul(
                            pb[bp + 1][:], selb[:, n * 128:(n + 1) * 128], ctsb[:],
                            start=True, stop=True), inc=("st", 1))
                    st_bc_h = {0: emit_bcast(0), 1: emit_bcast(1)}
                    for n in range(N):
                        nb = n % 2
                        bp = 4 + 2 * nb
                        p.wait(S, "st", st_bc_h[n])
                        if n >= 2:
                            p.wait(S, "sv", svW_l[(c, blk, n - 2, 3)])
                        p.emit(S, lambda e, nb=nb, bp=bp: e.copy(
                            out=brep[:, nb * 512:(nb + 1) * 512], in_=pb[bp][:]))
                        p.emit(S, lambda e, nb=nb, bp=bp: e.copy(
                            out=crep[:, nb * 512:(nb + 1) * 512], in_=pb[bp + 1][:]))
                        ssbce_h[n % 2] = s_inc()
                        if n + 2 < N:
                            st_bc_h[n + 2] = emit_bcast(n + 2)
                        for ei, et in enumerate(ets):
                            sa = viter % 2
                            sw = viter % 4
                            if viter >= 2:
                                pv = viter - 2
                                p.wait(S, "sv", svW_l[(c, blk, (pv) // 4, pv % 4)])
                                p.emit(S, lambda e, pet=ets[pv % 4], pn=pv // 4,
                                       psa=pv % 2: e.copy(
                                    out=hstate[:, pet * N + pn:pet * N + pn + 1],
                                    in_=ubuf[:, psa * 512 + 511:psa * 512 + 512]))
                            p.emit(S, lambda e, et=et, ei=ei, n=n, sa=sa: e.activation(
                                out=abar[:, sa * 512:(sa + 1) * 512],
                                in_=dl[:, ei * 512:(ei + 1) * 512], func=AF.Exp,
                                scale=acol[:, et * N + n:et * N + n + 1]))
                            ss_n = s_inc()
                            p.wait(V, "ss", ss_n)
                            if viter >= 4:
                                pv = viter - 4
                                p.wait(V, "st", styacc_l[(c, blk, pv // 4, pv % 4)])
                            p.emit(V, lambda e, ei=ei, nb=nb, sa=sa: e.tensor_tensor(
                                out=ubuf[:, sa * 512:(sa + 1) * 512],
                                in0=dx[:, ei * 512:(ei + 1) * 512],
                                in1=brep[:, nb * 512:(nb + 1) * 512], op=OP.mult))
                            p.emit(V, lambda e, et=et, n=n, sa=sa: e.tensor_tensor_scan(
                                out=ubuf[:, sa * 512:(sa + 1) * 512],
                                data0=abar[:, sa * 512:(sa + 1) * 512],
                                data1=ubuf[:, sa * 512:(sa + 1) * 512],
                                initial=hstate[:, et * N + n:et * N + n + 1],
                                op0=OP.mult, op1=OP.add))
                            p.emit(V, lambda e, nb=nb, sa=sa, sw=sw: e.tensor_tensor(
                                out=wbuf[:, sw * 512:(sw + 1) * 512],
                                in0=ubuf[:, sa * 512:(sa + 1) * 512],
                                in1=crep[:, nb * 512:(nb + 1) * 512], op=OP.mult))
                            svW_l[(c, blk, n, ei)] = v_inc()
                            p.wait(T, "sv", svW_l[(c, blk, n, ei)])
                            styacc_l[(c, blk, n, ei)] = p.emit(
                                T, lambda e, ei=ei, n=n, sw=sw: e.matmul(
                                    pb[ei][:], identb[:],
                                    wbuf[:, sw * 512:(sw + 1) * 512],
                                    start=False, stop=(n == N - 1),
                                    skip_group_check=True), inc=("st", 1))
                            viter += 1
                    p.wait(S, "sv", svW_l[(c, blk, N - 1, 3)])
                    for pv in (4 * N - 2, 4 * N - 1):
                        p.emit(S, lambda e, pet=ets[pv % 4], pn=pv // 4,
                               psa=pv % 2: e.copy(
                            out=hstate[:, pet * N + pn:pet * N + pn + 1],
                            in_=ubuf[:, psa * 512 + 511:psa * 512 + 512]))
                    for ei, et in enumerate(ets):
                        p.wait(V, "st", styacc_l[(c, blk, N - 1, ei)])
                        p.emit(V, lambda e, et=et, ei=ei: e.tensor_tensor(
                            out=xact[:, et * 512:(et + 1) * 512], in0=pb[ei][:],
                            in1=ss_bf[:, et * 512:(et + 1) * 512], op=OP.mult))
                        sv_y2[et] = v_inc()

                if c == 0:
                    dump("dl0", dl[:, 0:512])
                    dump("y2_0", xact[:, 0:512])
                    dump("y2_15", xact[:, 15 * 512:16 * 512])
    
            if PH >= 5:
                p.wait(T, "ss", p.cur("ss"))
                p.wait(T, "sv", sv_y2[ET - 1])
                sv_mix_last = 0
                for hf in range(2 if PH5 >= 1 else 0):
                    for et in range(ET):
                        wo_i = hf * ET + et
                        buf = wo_i % 4
                        p.wait(T, wo_vals[wo_i][0], wo_vals[wo_i][1])
                        p.wait(T, "sv", sv_y2[et])
                        if hf == 1 and et == 0:
                            p.wait(T, "sv", sv_mix_last)
                        for di in range(4 if OF >= 1 else 0):
                            p.emit(T, lambda e, et=et, buf=buf, hf=hf, di=di: e.matmul(
                                pb[di][:],
                                wo_t[:, buf * D + hf * 512 + di * 128:
                                     buf * D + hf * 512 + (di + 1) * 128],
                                xact[:, et * 512:(et + 1) * 512],
                                start=(et == 0), stop=(et == ET - 1)),
                                inc=("st", 1) if di == 3 else None)
                        wo_arr[buf] = p.cur("st")
                        if wo_i + 4 < 2 * ET:
                            load_wo(wo_i + 4)
                    if c == 0 and hf == 0:
                        dump("xq5b", xqT[:, 5 * 512:6 * 512])
                        dump("wqt5", wq_t[:, 5 * 512:6 * 512])
                    p.wait(T, "sv", sv_y2[-1])
                    for dt in range(DTl if OF >= 2 else 0):
                        wq_i = hf * DTl + dt
                        buf = wq_i % 8
                        p.wait(T, wqv[wq_i][0], wqv[wq_i][1])
                        for oi in range(4):
                            p.emit(T, lambda e, dt=dt, buf=buf, oi=oi: e.matmul(
                                pb[4 + oi][:],
                                wq_t[:, buf * 512 + oi * 128:buf * 512 + (oi + 1) * 128],
                                xqT[:, dt * 512:(dt + 1) * 512],
                                start=(dt == 0), stop=(dt == DTl - 1)),
                                inc=("st", 1) if oi == 3 else None)
                        wq_arr[buf] = p.cur("st")
                        if wq_i + 8 < 2 * DTl:
                            load_wq(wq_i + 8)
                    st_half = p.cur("st")
                    p.wait(V, "st", st_half)
                    st_tr_pair = 0
                    for pr in range(2 if OF >= 3 else 0):
                        if pr == 1:
                            p.wait(V, "st", st_tr_pair)
                        for sl in range(2):
                            di = pr * 2 + sl
                            dt_g = hf * 4 + di
                            p.emit(V, lambda e, di=di, dt_g=dt_g, sl=sl: e.scalar_tensor_tensor(
                                out=mxs[:, sl * 512:(sl + 1) * 512], in0=pb[4 + di][:],
                                scalar=fws_c[:, dt_g:dt_g + 1], in1=xsrep[:],
                                op0=OP.mult, op1=OP.mult))
                            p.emit(V, lambda e, di=di, dt_g=dt_g, sl=sl: e.scalar_tensor_tensor(
                                out=mxs[:, sl * 512:(sl + 1) * 512],
                                in0=mxs[:, sl * 512:(sl + 1) * 512],
                                scalar=fbias_c[:, dt_g:dt_g + 1], in1=pb[di][:],
                                op0=OP.add, op1=OP.add))
                        if c == 0 and hf == 0 and pr == 0:
                            dump("ffnpart", mxs[:, 0:512], block=(V,))
                        sv_mx = v_inc()
                        if OF >= 4:
                            p.wait(T, "sv", sv_mx)
                            for sl in range(2):
                                di = pr * 2 + sl
                                for tt in range(TT):
                                    p.emit(T, lambda e, di=di, sl=sl, tt=tt: e.transpose(
                                        pb[di][:, tt * 128:(tt + 1) * 128],
                                        mxs[:, sl * 512 + tt * 128:sl * 512 + (tt + 1) * 128],
                                        identf[:]),
                                        inc=("st", 1) if (sl == 1 and tt == TT - 1) else None)
                            st_tr_pair = p.cur("st")
                        if OF >= 5:
                            p.wait(V, "st", st_tr_pair)
                            for sl in range(2):
                                di = pr * 2 + sl
                                dt_g = hf * 4 + di
                                for tt in range(TT):
                                    xp = xba(par, tt)
                                    p.emit(V, lambda e, xp=xp, tt=tt, dt_g=dt_g: e.tensor_scalar(
                                        out=xp[:, dt_g * 128:(dt_g + 1) * 128],
                                        in0=xp[:, dt_g * 128:(dt_g + 1) * 128],
                                        scalar1=sc(11, tt), scalar2=sc(2, tt),
                                        op0=OP.mult, op1=OP.add))
                            sv_fts = v_inc()
                            p.wait(V, "sv", sv_fts)
                            for sl in range(2):
                                di = pr * 2 + sl
                                dt_g = hf * 4 + di
                                for tt in range(TT):
                                    xp = xba(par, tt)
                                    p.emit(V, lambda e, xp=xp, di=di, tt=tt, dt_g=dt_g: e.tensor_tensor(
                                        out=xp[:, dt_g * 128:(dt_g + 1) * 128],
                                        in0=xp[:, dt_g * 128:(dt_g + 1) * 128],
                                        in1=pb[di][:, tt * 128:(tt + 1) * 128], op=OP.add))
                            sv_mix_last = v_inc()
                sv_final = p.cur("sv")
                p.wait(SY, "sv", sv_final)
                for tt in range(TT):
                    p.emit(SY, lambda e, tt=tt, c=c, par=par: e.dma_start(
                        out=y_out[c * CH + tt * 128:c * CH + (tt + 1) * 128, :],
                        in_=xba(par, tt)), inc=("sox", 16))

        p.wait(SY, "sox", p.cur("sox"))
        for s in ("sv", "ss", "st", "gq"):
            p.wait(SY, s, p.cur(s))
        for s in ("sv", "ss", "st", "gq"):
            p.wait(SY, s, p.cur(s))

        with nc.Block() as block:
            @block.sync
            def _(sync):
                for fn in p.ops["sync"]:
                    fn(sync)

            @block.gpsimd
            def _(gpsimd):
                for fn in p.ops["gpsimd"]:
                    fn(gpsimd)

            @block.vector
            def _(vector):
                for fn in p.ops["vector"]:
                    fn(vector)

            @block.scalar
            def _(scalar):
                for fn in p.ops["scalar"]:
                    fn(scalar)

            @block.tensor
            def _(tensor):
                for fn in p.ops["tensor"]:
                    fn(tensor)

    return nc


def host_prep(inputs):
    g = lambda k: np.ascontiguousarray(inputs[k])
    resid = g("L0_attn_resid")
    skip_w = g("L0_attn_skip_w")
    in_w = g("L0_attn_in_w")
    out_w = g("L0_attn_out_w")
    wq = g("L0_ffn_weight_q")

    def colmaj(v, ntile):
        return np.ascontiguousarray(v.reshape(ntile, 128).T)

    shared = {
        "w_in": np.ascontiguousarray(in_w.T),
        "w_skip": np.ascontiguousarray(skip_w.T.reshape(D, ET, 128).transpose(1, 0, 2)),
        "w_out": np.ascontiguousarray(out_w.T.reshape(ET, 128, D)),
        "w_q": np.ascontiguousarray(wq.T),
        "wd1t": np.ascontiguousarray(
            g("L0_attn_wd1").T.reshape(ET, 128, DT).transpose(1, 0, 2)
            .reshape(128, ET * DT)),
        "wd2t": np.ascontiguousarray(g("L0_attn_wd2").T),
        "wbt": np.ascontiguousarray(
            g("L0_attn_wb").T.reshape(ET, 128, N).transpose(1, 0, 2)
            .reshape(128, ET * N)),
        "wct": np.ascontiguousarray(
            g("L0_attn_wc").T.reshape(ET, 128, N).transpose(1, 0, 2)
            .reshape(128, ET * N)),
        "alog": np.ascontiguousarray(
            g("L0_attn_A_log").reshape(ET, 128, N).transpose(1, 0, 2)
            .reshape(128, ET * N)),
        "normw": colmaj(g("L0_attn_norm_w"), DTl),
        "convw": np.ascontiguousarray(
            g("L0_attn_conv_w").reshape(ET, 128, K).transpose(1, 0, 2)
            .reshape(128, ET * K)),
        "convb": colmaj(g("L0_attn_conv_b"), ET),
        "wd2b": colmaj(g("L0_attn_wd2_b"), ET),
        "wdvec": colmaj(g("L0_attn_W_D"), ET),
        "fbias": colmaj(g("L0_ffn_bias"), DTl),
        "fws": colmaj(g("L0_ffn_weight_scale"), DTl),
    }
    return [dict(shared, x=np.ascontiguousarray(resid[b])) for b in range(8)]


def kernel(**inputs):
    in_maps = host_prep(inputs)
    nc = build_core()
    res = run_bass_kernel_spmd(nc, in_maps, core_ids=list(range(8)))
    return np.stack([r["y"] for r in res.results], axis=0).astype(np.float32)


# version 08 (final)
# speedup vs baseline: 1.1963x; 1.0117x over previous
import numpy as np
import ml_dtypes
import concourse.bass as bass
import concourse.mybir as mybir
from concourse.bass_utils import run_bass_kernel_spmd

F32 = mybir.dt.float32
BF16 = mybir.dt.bfloat16
I8 = mybir.dt.int8
AF = mybir.ActivationFunctionType
OP = mybir.AluOpType

L, D, E, N, DT, K = 4096, 1024, 2048, 16, 64, 4
CH = 512
NCH = L // CH
ET, DTl = E // 128, D // 128
TT = CH // 128
MAGIC = float(np.float32(12582912.0))
LN_EPS = 1e-5
RMS_EPS = 1e-5
Q_EPS = 1e-6


def _act_set_idx(nc, *funcs):
    from concourse.hw_specs import get_activation_tables
    tabs = get_activation_tables(nc.m.arch)
    for idx, (name, fs) in enumerate(tabs.items()):
        if all(f in fs for f in funcs):
            return idx
    raise ValueError(f"no table set with {funcs}")


class Plan:
    ENG = ("sync", "gpsimd", "vector", "scalar", "tensor")

    def __init__(self, semnames):
        self.ops = {e: [] for e in self.ENG}
        self.cnt = {s: 0 for s in semnames}
        self.sems = None

    def emit(self, eng, fn, inc=None):
        if inc is not None:
            sem, amt = inc
            self.cnt[sem] += amt
            def wrapped(e, fn=fn, sem=sem, amt=amt):
                fn(e).then_inc(self.sems[sem], amt)
            self.ops[eng].append(wrapped)
            return self.cnt[sem]
        self.ops[eng].append(fn)
        return None

    def wait(self, eng, sem, val):
        if val <= 0:
            return
        assert val <= self.cnt[sem], (eng, sem, val, self.cnt[sem])
        self.ops[eng].append(lambda e, s=sem, v=val: e.wait_ge(self.sems[s], v))

    def cur(self, sem):
        return self.cnt[sem]


def build_core(dbg_tags=()):
    import os
    NCH_RUN = int(os.environ.get("MK_NCH", NCH))
    PH = int(os.environ.get("MK_PH", 9))
    OF = int(os.environ.get("MK_OF", 9))
    PH5 = int(os.environ.get("MK_PH5", 9))
    NOWO = os.environ.get("MK_NOWO", "0") == "1"
    NOOMM = os.environ.get("MK_NOOMM", "0") == "1"
    nc = bass.Bass("TRN2", target_bir_lowering=False,
                   detect_race_conditions=False)
    SET_EXPLN = _act_set_idx(nc, AF.Exp, AF.Ln)
    SET_SILU = _act_set_idx(nc, AF.Silu)

    x_in = nc.dram_tensor("x", [L, D], F32, kind="ExternalInput").ap()
    y_out = nc.dram_tensor("y", [L, D], F32, kind="ExternalOutput").ap()
    w_in = nc.dram_tensor("w_in", [D, E], F32, kind="ExternalInput").ap()
    w_skip = nc.dram_tensor("w_skip", [ET, D, 128], F32, kind="ExternalInput").ap()
    w_out = nc.dram_tensor("w_out", [ET, 128, D], F32, kind="ExternalInput").ap()
    w_q = nc.dram_tensor("w_q", [D, D], I8, kind="ExternalInput").ap()
    wd1t = nc.dram_tensor("wd1t", [128, ET * DT], F32, kind="ExternalInput").ap()
    wd2t = nc.dram_tensor("wd2t", [DT, E], F32, kind="ExternalInput").ap()
    wbt = nc.dram_tensor("wbt", [128, ET * N], F32, kind="ExternalInput").ap()
    wct = nc.dram_tensor("wct", [128, ET * N], F32, kind="ExternalInput").ap()
    alog = nc.dram_tensor("alog", [128, ET * N], F32, kind="ExternalInput").ap()
    normw = nc.dram_tensor("normw", [128, DTl], F32, kind="ExternalInput").ap()
    convw = nc.dram_tensor("convw", [128, ET * K], F32, kind="ExternalInput").ap()
    convb = nc.dram_tensor("convb", [128, ET], F32, kind="ExternalInput").ap()
    wd2b = nc.dram_tensor("wd2b", [128, ET], F32, kind="ExternalInput").ap()
    wdvec = nc.dram_tensor("wdvec", [128, ET], F32, kind="ExternalInput").ap()
    fbias = nc.dram_tensor("fbias", [128, DTl], F32, kind="ExternalInput").ap()
    fws = nc.dram_tensor("fws", [128, DTl], F32, kind="ExternalInput").ap()

    wc_in = nc.dram_tensor("wc_in", [4, DTl, 128, 512], BF16, kind="Internal").ap()
    wc_skip = nc.dram_tensor("wc_skip", [ET, DTl, 128, 128], BF16, kind="Internal").ap()
    wc_out = nc.dram_tensor("wc_out", [ET, 128, D], BF16, kind="Internal").ap()
    wc_wq = nc.dram_tensor("wc_wq", [2, DTl, 128, 512], BF16, kind="Internal").ap()

    ident_c = nc.inline_tensor(np.eye(128, dtype=np.float32), "identf_c").ap()
    onesb_c = nc.inline_tensor(np.ones((1, 128), dtype=ml_dtypes.bfloat16), "onesb_c").ap()
    _sel = np.zeros((16, 16 * 128), dtype=ml_dtypes.bfloat16)
    for _n in range(16):
        _sel[_n, _n * 128:(_n + 1) * 128] = 1.0
    selb_c = nc.inline_tensor(_sel, "selb_c").ap()
    onesf_c = nc.inline_tensor(np.ones((1, 128), dtype=np.float32), "onesf_c").ap()

    dbg = {}
    for tag, shape in dbg_tags:
        dbg[tag] = nc.dram_tensor("dbg_" + tag, list(shape), F32,
                                  kind="ExternalOutput").ap()

    from contextlib import ExitStack
    with ExitStack() as stack:
        sbt = lambda n, s, d: stack.enter_context(nc.sbuf_tensor(n, s, d))
        pst = lambda n: stack.enter_context(nc.psum_tensor(n, [128, 512], F32))

        wd1_bf = sbt("wd1_bf", [128, ET * DT], BF16)
        wd2_bf = sbt("wd2_bf", [DT, E], BF16)
        wb_bf = sbt("wb_bf", [128, ET * N], BF16)
        wc_bf = sbt("wc_bf", [128, ET * N], BF16)
        acol = sbt("acol", [128, ET * N], F32)
        normw_c = sbt("normw_c", [128, DTl], F32)
        convb_c = sbt("convb_c", [128, ET], F32)
        wd2b_c = sbt("wd2b_c", [128, ET], F32)
        fbias_c = sbt("fbias_c", [128, DTl], F32)
        fws_c = sbt("fws_c", [128, DTl], F32)
        convw_c = sbt("convw_c", [128, ET * K], F32)
        wdvec_c = sbt("wdvec_c", [128, ET], F32)
        identf = sbt("identf", [128, 128], F32)
        identb = sbt("identb", [128, 128], BF16)
        onesb = sbt("onesb", [1, 128], BF16)
        selb = sbt("selb", [16, 16 * 128], BF16)
        onesf = sbt("onesf", [1, 128], F32)
        cdiag = sbt("cdiag", [128, ET * K * 128], BF16)
        wdiag = sbt("wdiag", [128, ET * 128], BF16)
        hstate = sbt("hstate", [128, ET * N], F32)
        halo = sbt("halo", [128, ET * 3], BF16)
        epsc = sbt("epsc", [128, 1], F32)
        onec = sbt("onec", [128, 1], F32)

        win_q = sbt("win_q", [128, 2 * DTl * 512], BF16)
        wsk_t = sbt("wsk_t", [128, 4 * DTl * 128], BF16)
        wo_t = sbt("wo_t", [128, 4 * D], BF16)
        wq_t = sbt("wq_t", [128, 8 * 512], BF16)

        xbuf = sbt("xbuf", [128, 2 * TT * D], F32)
        sqtmp = sbt("sqtmp", [128, D], F32)
        xnsf = sbt("xnsf", [128, D], F32)
        xnsT = sbt("xnsT", [128, DTl * 512], BF16)
        xqT = sbt("xqT", [128, DTl * 512], BF16)
        xact = sbt("xact", [128, ET * 512], BF16)
        ss_bf = sbt("ss_bf", [128, ET * 512], BF16)
        xtl = sbt("xtl", [128, 2 * 515], BF16)
        dl = sbt("dl", [128, 4 * 512], F32)
        dx = sbt("dx", [128, 4 * 512], BF16)
        t1f = sbt("t1f", [128, 2 * 512], F32)
        abar = sbt("abar", [128, 2 * 512], F32)
        ubuf = sbt("ubuf", [128, 2 * 512], BF16)
        wbuf = sbt("wbuf", [128, 4 * 512], BF16)
        brep = sbt("brep", [128, 2 * 512], BF16)
        crep = sbt("crep", [128, 2 * 512], BF16)
        btsb = sbt("btsb", [16, 512], BF16)
        ctsb = sbt("ctsb", [16, 512], BF16)
        z1bf = sbt("z1bf", [DT, 512], BF16)
        xsr = sbt("xsr", [1, TT * 128], F32)
        xsrep = sbt("xsrep", [128, 512], F32)
        mxs = sbt("mxs", [128, 2 * 512], F32)
        statc = sbt("statc", [128, TT * 16], F32)

        pb = [pst(f"pb{i}") for i in range(8)]

        semnames = (["sq", "gq", "sv", "ss", "st", "sqp", "sox", "sx0", "sx1"]
                    + [f"win{i}" for i in range(2)]
                    + [f"wsk{i}" for i in range(4)]
                    + [f"wo{i}" for i in range(4)]
                    + [f"wq{i}" for i in range(8)])
        p = Plan(semnames)
        sems = {}
        for sname in semnames:
            sems[sname] = stack.enter_context(nc.semaphore(sname))
        p.sems = sems

        V, S, T, SY, G = "vector", "scalar", "tensor", "sync", "gpsimd"

        def _attach(eng, sem):
            fn = p.ops[eng].pop()
            p.cnt[sem] += 1
            def wrapped(e, fn=fn, sem=sem):
                fn(e).then_inc(p.sems[sem], 1)
            p.ops[eng].append(wrapped)
            return p.cnt[sem]

        def v_inc():
            return _attach(V, "sv")

        def s_inc():
            return _attach(S, "ss")


        def sc(col, tt):
            return statc[:, tt * 16 + col:tt * 16 + col + 1]

        def xba(par, tt):
            return xbuf[:, (par * TT + tt) * D:(par * TT + tt + 1) * D]

        def dump(tag, ap, block=()):
            if tag not in dbg:
                return
            for s in ("sv", "ss", "st"):
                p.wait(G, s, p.cur(s))
            p.emit(G, lambda e, ap=ap, tag=tag: e.dma_start(
                out=dbg[tag], in_=ap), inc=("gq", 16))
            p.wait(G, "gq", p.cur("gq"))
            for eng in block:
                p.wait(eng, "gq", p.cur("gq"))

        wd1_st = xbuf[:, 0:ET * DT]
        wd2_st = xbuf[0:DT, 2048:2048 + E]
        wb_st = xbuf[:, 4096:4096 + ET * N]
        wc_st = xbuf[:, 4352:4352 + ET * N]
        alog_st = xbuf[:, 4608:4608 + ET * N]

        for src, dst in [(wd1t, wd1_st), (wd2t, wd2_st), (wbt, wb_st),
                         (wct, wc_st), (alog, alog_st)]:
            p.emit(SY, lambda e, s=src, d=dst: e.dma_start(out=d, in_=s),
                   inc=("sqp", 16))
        for src, dst in [(normw, normw_c[:]), (convw, convw_c[:]),
                         (convb, convb_c[:]), (wd2b, wd2b_c[:]),
                         (wdvec, wdvec_c[:]), (fbias, fbias_c[:]),
                         (fws, fws_c[:]), (ident_c, identf[:]),
                         (onesb_c, onesb[:]), (onesf_c, onesf[:]),
                         (selb_c, selb[:])]:
            p.emit(SY, lambda e, s=src, d=dst: e.dma_start(out=d, in_=s),
                   inc=("sqp", 16))
        sq_small = p.cur("sqp")

        p.wait(V, "sqp", sq_small)
        p.emit(V, lambda e: e.tensor_copy(out=wd1_bf[:], in_=wd1_st))
        p.emit(V, lambda e: e.tensor_copy(out=wd2_bf[:], in_=wd2_st))
        p.emit(V, lambda e: e.tensor_copy(out=wb_bf[:], in_=wb_st))
        p.emit(V, lambda e: e.tensor_copy(out=wc_bf[:], in_=wc_st))
        p.emit(V, lambda e: e.tensor_copy(out=identb[:], in_=identf[:]))
        sv_cp = v_inc()
        p.wait(S, "sv", sv_cp)
        p.emit(S, lambda e: e.activation(out=acol[:], in_=alog_st, func=AF.Exp))
        ss_acol = s_inc()
        p.wait(V, "ss", ss_acol)
        p.emit(V, lambda e: e.tensor_scalar(out=acol[:], in0=acol[:], scalar1=-1.0,
                                            scalar2=None, op0=OP.mult))
        for et in range(ET):
            for k in range(K):
                p.emit(V, lambda e, et=et, k=k: e.tensor_scalar(
                    out=cdiag[:, (et * K + k) * 128:(et * K + k + 1) * 128],
                    in0=identf[:], scalar1=convw_c[:, et * K + k:et * K + k + 1],
                    scalar2=None, op0=OP.mult))
            p.emit(V, lambda e, et=et: e.tensor_scalar(
                out=wdiag[:, et * 128:(et + 1) * 128],
                in0=identf[:], scalar1=wdvec_c[:, et:et + 1],
                scalar2=None, op0=OP.mult))
        p.emit(V, lambda e: e.memset(epsc[:], LN_EPS))
        p.emit(V, lambda e: e.memset(onec[:], 1.0))
        p.emit(V, lambda e: e.memset(hstate[:], 0.0))
        p.emit(V, lambda e: e.memset(halo[:], 0.0))
        p.emit(V, lambda e: e.memset(statc[:], 0.0))
        sv_const = v_inc()

        last_fold_v = sv_const
        last_out_q = 0
        stg_f = xbuf[:, 0:E]

        def cache_piece(dma_in_fn, fold_fn, dma_out_fns):
            nonlocal last_fold_v, last_out_q
            p.wait(G, "sv", last_fold_v)
            gv = p.emit(G, dma_in_fn, inc=("gq", 16))
            p.wait(V, "gq", gv)
            p.wait(V, "sq", last_out_q)
            fold_fn()
            last_fold_v = v_inc()
            p.wait(SY, "sv", last_fold_v)
            for fn in dma_out_fns:
                p.emit(SY, fn, inc=("sq", 16))
            last_out_q = p.cur("sq")

        for dt in range(DTl):
            cache_piece(
                lambda e, dt=dt: e.dma_start(
                    out=stg_f, in_=w_in[dt * 128:(dt + 1) * 128, :]),
                lambda dt=dt: p.emit(V, lambda e, dt=dt: e.tensor_scalar(
                    out=xact[:, 0:E], in0=stg_f,
                    scalar1=normw_c[:, dt:dt + 1], scalar2=None, op0=OP.mult)),
                [lambda e, g=g, dt=dt: e.dma_start(
                    out=wc_in[g, dt], in_=xact[:, g * 512:(g + 1) * 512])
                 for g in range(4)])
        for et in range(ET):
            def fold_skip(et=et):
                for dt in range(DTl):
                    p.emit(V, lambda e, dt=dt: e.tensor_scalar(
                        out=xact[:, dt * 128:(dt + 1) * 128],
                        in0=xbuf[:, dt * 128:(dt + 1) * 128],
                        scalar1=normw_c[:, dt:dt + 1], scalar2=None, op0=OP.mult))
            cache_piece(
                lambda e, et=et: e.dma_start(
                    out=xbuf[:, 0:DTl * 128].rearrange("p (dt e) -> p dt e", e=128),
                    in_=w_skip[et].rearrange("(dt p) e -> p dt e", p=128)),
                fold_skip,
                [lambda e, et=et: e.dma_start(
                    out=wc_skip[et].rearrange("dt p e -> p dt e"),
                    in_=xact[:, 0:DTl * 128].rearrange("p (dt e) -> p dt e", e=128))])
        for et in range(ET):
            cache_piece(
                lambda e, et=et: e.dma_start(out=xbuf[:, 0:D], in_=w_out[et]),
                lambda: p.emit(V, lambda e: e.tensor_copy(
                    out=xact[:, 0:D], in_=xbuf[:, 0:D])),
                [lambda e, et=et: e.dma_start(out=wc_out[et], in_=xact[:, 0:D])])
        for dt in range(DTl):
            cache_piece(
                lambda e, dt=dt: e.dma_start(
                    out=xbuf[:, 0:D // 4].bitcast(I8),
                    in_=w_q[dt * 128:(dt + 1) * 128, :]),
                lambda: p.emit(V, lambda e: e.tensor_copy(
                    out=xact[:, 0:D], in_=xbuf[:, 0:D // 4].bitcast(I8))),
                [lambda e, hf=hf, dt=dt: e.dma_start(
                    out=wc_wq[hf, dt], in_=xact[:, hf * 512:(hf + 1) * 512])
                 for hf in range(2)])
        sq_cache = p.cur("sq")

        p.wait(SY, "sq", sq_cache)
        for tt in range(TT):
            p.emit(SY, lambda e, tt=tt: e.dma_start(
                out=xba(0, tt), in_=x_in[tt * 128:(tt + 1) * 128, :]),
                inc=("sx0", 16))
        xin_done = [("sx0", p.cur("sx0"))]

        winq_arr = [0, 0]
        wsk_arr = [0] * 4
        wo_arr = [0] * 4
        wq_arr = [0] * 8
        st_trT = 0
        svW_l = {}
        ssbce_h = [0, 0]
        styacc_l = {}
        ssbce_l = {}
        sv_y2_prev_last = 0

        for c in range(NCH_RUN):
            par = c % 2

            if c + 1 < NCH_RUN:
                sxn = "sx%d" % ((c + 1) % 2)
                p.wait(SY, "sox", p.cur("sox"))
                for tt in range(TT):
                    p.emit(SY, lambda e, tt=tt, c=c: e.dma_start(
                        out=xba((c + 1) % 2, tt),
                        in_=x_in[(c + 1) * CH + tt * 128:(c + 1) * CH + (tt + 1) * 128, :]),
                        inc=(sxn, 16))
                xin_done.append((sxn, p.cur(sxn)))

            winq_vals = [None] * 4
            wsk_vals = [None] * ET
            wo_vals = [None] * (2 * ET)
            wqv = [None] * (2 * DTl)

            def load_win(g):
                buf = g % 2
                p.wait(G, "st", winq_arr[buf])
                v = p.emit(G, lambda e, g=g, buf=buf: e.dma_start(
                    out=win_q[:, buf * DTl * 512:(buf + 1) * DTl * 512]
                        .rearrange("p (dt e) -> p dt e", e=512),
                    in_=wc_in[g].rearrange("dt p e -> p dt e")),
                    inc=("win%d" % buf, 16))
                winq_vals[g] = ("win%d" % buf, v)

            def load_wsk(et):
                buf = et % 4
                p.wait(G, "st", wsk_arr[buf])
                v = p.emit(G, lambda e, et=et, buf=buf: e.dma_start(
                    out=wsk_t[:, buf * DTl * 128:(buf + 1) * DTl * 128]
                        .rearrange("p (dt e) -> p dt e", e=128),
                    in_=wc_skip[et].rearrange("dt p e -> p dt e")),
                    inc=("wsk%d" % buf, 16))
                wsk_vals[et] = ("wsk%d" % buf, v)

            def load_wo(i):
                if NOWO:
                    wo_vals[i] = ("wo0", 0)
                    return
                et = i % ET
                buf = i % 4
                p.wait(G, "st", wo_arr[buf])
                v = p.emit(G, lambda e, et=et, buf=buf: e.dma_start(
                    out=wo_t[:, buf * D:(buf + 1) * D], in_=wc_out[et]),
                    inc=("wo%d" % buf, 16))
                wo_vals[i] = ("wo%d" % buf, v)

            def load_wq(i):
                hf, dt = divmod(i, DTl)
                buf = i % 8
                p.wait(G, "st", wq_arr[buf])
                v = p.emit(G, lambda e, hf=hf, dt=dt, buf=buf: e.dma_start(
                    out=wq_t[:, buf * 512:(buf + 1) * 512], in_=wc_wq[hf, dt]),
                    inc=("wq%d" % buf, 16))
                wqv[i] = ("wq%d" % buf, v)

            load_win(0)
            load_win(1)
            for et in range(4):
                load_wsk(et)
            for i in range(4):
                load_wo(i)
            for i in range(8):
                load_wq(i)

            if PH >= 1:
                p.wait(V, xin_done[c][0], xin_done[c][1])
                tr_evac_s = []
                p.wait(V, "st", st_trT)
                for tt in range(TT):
                    xp = xba(par, tt)
                    sqh = sqtmp[:, 0:D] if tt % 2 == 0 else xnsf[:, 0:D]
                    p.emit(V, lambda e, xp=xp, sqh=sqh: e.tensor_tensor(
                        out=sqh, in0=xp, in1=xp, op=OP.mult))
                    p.emit(V, lambda e, xp=xp, tt=tt: e.tensor_reduce(
                        out=sc(0, tt), in_=xp, axis=mybir.AxisListType.X, op=OP.add))
                    p.emit(V, lambda e, tt=tt, sqh=sqh: e.tensor_reduce(
                        out=sc(1, tt), in_=sqh,
                        axis=mybir.AxisListType.X, op=OP.add))
                sv_st = v_inc()
                p.wait(V, "sv", sv_st)
                for tt in range(TT):
                    p.emit(V, lambda e, tt=tt: e.tensor_scalar(
                        out=sc(2, tt), in0=sc(0, tt), scalar1=1.0 / D, scalar2=None,
                        op0=OP.mult))
                sv_st = v_inc()
                p.wait(V, "sv", sv_st)
                for tt in range(TT):
                    p.emit(V, lambda e, tt=tt: e.tensor_tensor(
                        out=sc(12, tt), in0=sc(2, tt), in1=sc(2, tt), op=OP.mult))
                    p.emit(V, lambda e, tt=tt: e.tensor_scalar(
                        out=sc(3, tt), in0=sc(1, tt), scalar1=1.0 / D, scalar2=None,
                        op0=OP.mult))
                sv_st = v_inc()
                p.wait(V, "sv", sv_st)
                for tt in range(TT):
                    p.emit(V, lambda e, tt=tt: e.tensor_tensor(
                        out=sc(3, tt), in0=sc(3, tt), in1=sc(12, tt), op=OP.subtract))
                sv_st = v_inc()
                p.wait(S, "sv", sv_st)
                for tt in range(TT):
                    p.emit(S, lambda e, tt=tt: e.activation(
                        out=sc(12, tt), in_=sc(3, tt), func=AF.Ln, bias=epsc[:, 0:1]))
                for tt in range(TT):
                    p.emit(S, lambda e, tt=tt: e.activation(
                        out=sc(4, tt), in_=sc(12, tt), func=AF.Exp, scale=-0.5))
                ss_st = s_inc()
                p.wait(V, "ss", ss_st)
                for tt in range(TT):
                    p.emit(V, lambda e, tt=tt: e.tensor_tensor(
                        out=sc(13, tt), in0=sc(4, tt), in1=sc(4, tt), op=OP.mult))
                sv_st = v_inc()
                p.wait(V, "sv", sv_st)
                for tt in range(TT):
                    p.emit(V, lambda e, tt=tt: e.tensor_tensor(
                        out=sc(13, tt), in0=sc(13, tt), in1=sc(3, tt), op=OP.mult))
                sv_st = v_inc()
                p.wait(S, "sv", sv_st)
                for tt in range(TT):
                    p.emit(S, lambda e, tt=tt: e.activation(
                        out=sc(14, tt), in_=sc(13, tt), func=AF.Ln, bias=epsc[:, 0:1]))
                for tt in range(TT):
                    p.emit(S, lambda e, tt=tt: e.activation(
                        out=sc(5, tt), in_=sc(14, tt), func=AF.Exp, scale=-0.5))
                ss_st = s_inc()
                p.wait(V, "ss", ss_st)
                for tt in range(TT):
                    p.emit(V, lambda e, tt=tt: e.tensor_tensor(
                        out=sc(6, tt), in0=sc(4, tt), in1=sc(5, tt), op=OP.mult))
                    p.emit(V, lambda e, tt=tt: e.tensor_scalar(
                        out=sc(11, tt), in0=sc(4, tt), scalar1=1.0, scalar2=None,
                        op0=OP.add))
                    p.emit(V, lambda e, xp=xba(par, tt), tt=tt: e.tensor_scalar(
                        out=xp, in0=xp, scalar1=sc(2, tt), scalar2=None,
                        op0=OP.subtract))
                v_inc()
                for tt in range(TT):
                    p.emit(V, lambda e, xp=xba(par, tt), tt=tt: e.tensor_reduce(
                        out=sc(7, tt), in_=xp, axis=mybir.AxisListType.X, op=OP.max,
                        apply_absolute_value=True))
                sv_st = v_inc()
                p.wait(V, "sv", sv_st)
                for tt in range(TT):
                    p.emit(V, lambda e, tt=tt: e.tensor_tensor(
                        out=sc(8, tt), in0=sc(7, tt), in1=sc(4, tt), op=OP.mult))
                sv_st = v_inc()
                p.wait(V, "sv", sv_st)
                for tt in range(TT):
                    p.emit(V, lambda e, tt=tt: e.tensor_scalar(
                        out=sc(8, tt), in0=sc(8, tt), scalar1=Q_EPS, scalar2=1.0 / 127.0,
                        op0=OP.max, op1=OP.mult))
                sv_st = v_inc()
                p.wait(V, "sv", sv_st)
                for tt in range(TT):
                    p.emit(V, lambda e, tt=tt: e.reciprocal(out=sc(9, tt), in_=sc(8, tt)))
                sv_st = v_inc()
                p.wait(V, "sv", sv_st)
                for tt in range(TT):
                    p.emit(V, lambda e, tt=tt: e.tensor_tensor(
                        out=sc(10, tt), in0=sc(4, tt), in1=sc(9, tt), op=OP.mult))
                sv_st = v_inc()
                p.wait(V, "sv", sv_st)
                st_tr_q = 0
                for tt in range(TT):
                    xp = xba(par, tt)
                    p.wait(V, "st", st_tr_q)
                    p.emit(V, lambda e, xp=xp, tt=tt: e.tensor_scalar(
                        out=sqtmp[:, 0:D], in0=xp, scalar1=sc(10, tt), scalar2=MAGIC,
                        op0=OP.mult, op1=OP.add))
                    p.emit(V, lambda e: e.tensor_scalar(
                        out=sqtmp[:, 0:D], in0=sqtmp[:, 0:D], scalar1=MAGIC, scalar2=None,
                        op0=OP.subtract))
                    p.emit(V, lambda e, xp=xp, tt=tt: e.tensor_scalar(
                        out=xnsf[:], in0=xp, scalar1=sc(6, tt), scalar2=None,
                        op0=OP.mult))
                    vv = v_inc()
                    p.wait(T, "sv", vv)
                    for j in range(2 * DTl):
                        src = sqtmp if j < DTl else xnsf
                        dt = j % DTl
                        dst = xqT if j < DTl else xnsT
                        slot = j % 4
                        if len(tr_evac_s) >= 4:
                            p.wait(T, "ss", tr_evac_s[-4])
                        stv = p.emit(T, lambda e, src=src, dt=dt, slot=slot: e.transpose(
                            pb[slot][:, 0:128], src[:, dt * 128:(dt + 1) * 128],
                            identf[:]), inc=("st", 1))
                        p.wait(S, "st", stv)
                        p.emit(S, lambda e, dst=dst, dt=dt, tt=tt, slot=slot: e.copy(
                            out=dst[:, dt * 512 + tt * 128:dt * 512 + (tt + 1) * 128],
                            in_=pb[slot][:, 0:128]))
                        tr_evac_s.append(s_inc())
                    st_tr_q = p.cur("st")
                    st_trT = p.cur("st")
                ss_tphase = p.cur("ss")
    
                if c == 0:
                    dump("xq5a", xqT[:, 5 * 512:6 * 512])
                    dump("xq0", xqT[:, 0:512])
                    dump("xns0", xnsT[:, 0:512])
                    dump("stat0", statc[:])
    
            if PH >= 2:
                ss_grp_evac = [0] * 4
                for g in range(4):
                    buf = g % 2
                    p.wait(T, winq_vals[g][0], winq_vals[g][1])
                    p.wait(T, "ss", ss_tphase)
                    if g >= 1:
                        p.wait(T, "ss", ss_grp_evac[g - 1])
                    for dt in range(DTl):
                        for ei in range(4):
                            p.emit(T, lambda e, dt=dt, ei=ei, buf=buf: e.matmul(
                                pb[ei][:],
                                win_q[:, buf * DTl * 512 + dt * 512 + ei * 128:
                                      buf * DTl * 512 + dt * 512 + (ei + 1) * 128],
                                xnsT[:, dt * 512:(dt + 1) * 512],
                                start=(dt == 0), stop=(dt == DTl - 1)),
                                inc=("st", 1) if (dt == DTl - 1 and ei == 3) else None)
                    winq_arr[buf] = p.cur("st")
                    if g + 2 < 4:
                        load_win(g + 2)
                    st_inproj = p.cur("st")
                    for ei in range(4):
                        et = g * 4 + ei
                        ep = et % 2
                        p.wait(S, "st", st_inproj)
                        p.emit(S, lambda e, et=et, ep=ep: e.copy(
                            out=xtl[:, ep * 515 + 0:ep * 515 + 3],
                            in_=halo[:, et * 3:(et + 1) * 3]))
                        p.emit(S, lambda e, ei=ei, ep=ep: e.copy(
                            out=xtl[:, ep * 515 + 3:ep * 515 + 515], in_=pb[ei][:]))
                        ssv = s_inc()
                        if ei == 3:
                            ss_grp_evac[g] = ssv
                        p.wait(T, "ss", ssv)
                        for k in range(K):
                            p.emit(T, lambda e, et=et, k=k, ep=ep: e.matmul(
                                pb[4 + ep][:],
                                cdiag[:, (et * K + k) * 128:(et * K + k + 1) * 128],
                                xtl[:, ep * 515 + k:ep * 515 + k + 512],
                                start=(k == 0), stop=(k == K - 1)),
                                inc=("st", 1) if k == K - 1 else None)
                        stv = p.cur("st")
                        p.wait(S, "st", stv)
                        p.emit(S, lambda e, et=et, ep=ep: e.activation(
                            out=xact[:, et * 512:(et + 1) * 512],
                            in_=pb[4 + ep][:], func=AF.Silu,
                            bias=convb_c[:, et:et + 1]))
                        p.emit(S, lambda e, et=et, ep=ep: e.copy(
                            out=halo[:, et * 3:(et + 1) * 3],
                            in_=xtl[:, ep * 515 + 512:ep * 515 + 515]))
                        s_inc()
                        p.wait(T, wsk_vals[et][0], wsk_vals[et][1])
                        for dt in range(DTl):
                            p.emit(T, lambda e, et=et, dt=dt, ep=ep: e.matmul(
                                pb[6 + ep][:],
                                wsk_t[:, (et % 4) * DTl * 128 + dt * 128:
                                      (et % 4) * DTl * 128 + (dt + 1) * 128],
                                xnsT[:, dt * 512:(dt + 1) * 512],
                                start=(dt == 0), stop=(dt == DTl - 1)),
                                inc=("st", 1) if dt == DTl - 1 else None)
                        wsk_arr[et % 4] = p.cur("st")
                        if et + 4 < ET:
                            load_wsk(et + 4)
                        stv = p.cur("st")
                        p.wait(S, "st", stv)
                        p.emit(S, lambda e, et=et, ep=ep: e.activation(
                            out=ss_bf[:, et * 512:(et + 1) * 512],
                            in_=pb[6 + ep][:], func=AF.Silu))
                        s_inc()
                ss_pphase = p.cur("ss")
    
                if c == 0:
                    dump("xact0", xact[:, 0:512])
                    dump("ssbf0", ss_bf[:, 0:512])
    
            if PH >= 3:
                p.wait(T, "ss", ss_pphase)
                for et in range(ET):
                    p.emit(T, lambda e, et=et: e.matmul(
                        pb[0][0:DT, :], wd1_bf[:, et * DT:(et + 1) * DT],
                        xact[:, et * 512:(et + 1) * 512],
                        start=(et == 0), stop=(et == ET - 1)),
                        inc=("st", 1) if et == ET - 1 else None)
                for et in range(ET):
                    p.emit(T, lambda e, et=et: e.matmul(
                        pb[1][0:N, :], wb_bf[:, et * N:(et + 1) * N],
                        xact[:, et * 512:(et + 1) * 512],
                        start=(et == 0), stop=(et == ET - 1)),
                        inc=("st", 1) if et == ET - 1 else None)
                for et in range(ET):
                    p.emit(T, lambda e, et=et: e.matmul(
                        pb[2][0:N, :], wc_bf[:, et * N:(et + 1) * N],
                        xact[:, et * 512:(et + 1) * 512],
                        start=(et == 0), stop=(et == ET - 1)),
                        inc=("st", 1) if et == ET - 1 else None)
                st_small = p.cur("st")
                p.wait(V, "st", st_small)
                p.emit(V, lambda e: e.tensor_copy(out=z1bf[:], in_=pb[0][0:DT, :]))
                p.emit(V, lambda e: e.tensor_copy(out=btsb[:], in_=pb[1][0:N, :]))
                p.emit(V, lambda e: e.tensor_copy(out=ctsb[:], in_=pb[2][0:N, :]))
                p.emit(V, lambda e: e.tensor_copy(
                    out=sqtmp[:, 0:TT],
                    in_=statc[:].rearrange("p (t c) -> p t c", c=16)[:, :, 8]))
                sv_z1 = v_inc()
                p.wait(T, "sv", sv_z1)
                for tt in range(TT):
                    p.emit(T, lambda e, tt=tt: e.transpose(
                        pb[3][0:1, tt * 128:(tt + 1) * 128], sqtmp[:, tt:tt + 1],
                        identf[:]), inc=("st", 1) if tt == TT - 1 else None)
                stv = p.cur("st")
                p.wait(S, "st", stv)
                p.emit(S, lambda e: e.copy(out=xsr[:], in_=pb[3][0:1, 0:TT * 128]))
                ss_xsr = s_inc()
                p.wait(T, "ss", ss_xsr)
                for tt in range(TT):
                    p.emit(T, lambda e, tt=tt: e.matmul(
                        pb[3][:, tt * 128:(tt + 1) * 128], onesf[:],
                        xsr[0:1, tt * 128:(tt + 1) * 128], start=True, stop=True),
                        inc=("st", 1) if tt == TT - 1 else None)
                stv = p.cur("st")
                p.wait(S, "st", stv)
                p.emit(S, lambda e: e.copy(out=xsrep[:], in_=pb[3][:]))
                ss_xsrep = s_inc()
    
                if c == 0:
                    dump("z1", z1bf[:])
                    dump("bt", btsb[:])
                    dump("xsrep", xsrep[:])
    
            if PH >= 4:
                sv_y2 = [0] * ET
                for blk in range(4):
                    ets = [blk * 4 + i for i in range(4)]
                    ss_sp_of = {}
                    for ei, et in enumerate(ets):
                        wp = 6 + (ei % 2)
                        p.wait(T, "sv", sv_z1)
                        if blk > 0:
                            p.wait(T, "sv", sv_y2[(blk - 1) * 4 + ei])
                        if ei >= 2:
                            p.wait(T, "ss", ss_sp_of[ei - 2])
                        p.wait(T, "ss", ssbce_h[1])
                        p.emit(T, lambda e, et=et, wp=wp: e.matmul(
                            pb[wp][:], wd2_bf[:, et * 128:(et + 1) * 128], z1bf[:],
                            start=True, stop=True), inc=("st", 1))
                        st_dpre = p.cur("st")
                        p.wait(S, "st", st_dpre)
                        p.emit(S, lambda e, et=et, ei=ei, wp=wp: e.activation(
                            out=t1f[:, (ei % 2) * 512:(ei % 2) * 512 + 512], in_=pb[wp][:],
                            func=AF.Exp, bias=wd2b_c[:, et:et + 1]))
                        ss_sp = s_inc()
                        ss_sp_of[ei] = ss_sp
                        p.emit(S, lambda e, ei=ei: e.activation(
                            out=dl[:, ei * 512:(ei + 1) * 512],
                            in_=t1f[:, (ei % 2) * 512:(ei % 2) * 512 + 512], func=AF.Ln,
                            bias=onec[:, 0:1]))
                        ss_dl = s_inc()
                        p.wait(V, "ss", ss_dl)
                        p.emit(V, lambda e, et=et, ei=ei: e.tensor_tensor(
                            out=dx[:, ei * 512:(ei + 1) * 512],
                            in0=dl[:, ei * 512:(ei + 1) * 512],
                            in1=xact[:, et * 512:(et + 1) * 512], op=OP.mult))
                        v_inc()
                        p.emit(T, lambda e, et=et, ei=ei: e.matmul(
                            pb[ei][:], wdiag[:, et * 128:(et + 1) * 128],
                            xact[:, et * 512:(et + 1) * 512], start=True, stop=False,
                            skip_group_check=True))
                    viter = 0
                    def emit_bcast(n, blk=blk):
                        bp = 4 + 2 * (n % 2)
                        p.wait(T, "ss", ssbce_h[n % 2])
                        if n % 2 == 1:
                            p.wait(T, "ss", ss_sp_of[3])
                        p.emit(T, lambda e, n=n, bp=bp: e.matmul(
                            pb[bp][:], selb[:, n * 128:(n + 1) * 128], btsb[:],
                            start=True, stop=True))
                        return p.emit(T, lambda e, n=n, bp=bp: e.matmul(
                            pb[bp + 1][:], selb[:, n * 128:(n + 1) * 128], ctsb[:],
                            start=True, stop=True), inc=("st", 1))
                    st_bc_h = {0: emit_bcast(0), 1: emit_bcast(1)}
                    for n in range(N):
                        nb = n % 2
                        bp = 4 + 2 * nb
                        p.wait(S, "st", st_bc_h[n])
                        if n >= 2:
                            p.wait(S, "sv", svW_l[(c, blk, n - 2, 3)])
                        p.emit(S, lambda e, nb=nb, bp=bp: e.copy(
                            out=brep[:, nb * 512:(nb + 1) * 512], in_=pb[bp][:]))
                        p.emit(S, lambda e, nb=nb, bp=bp: e.copy(
                            out=crep[:, nb * 512:(nb + 1) * 512], in_=pb[bp + 1][:]))
                        ssbce_h[n % 2] = s_inc()
                        if n + 2 < N:
                            st_bc_h[n + 2] = emit_bcast(n + 2)
                        for ei, et in enumerate(ets):
                            sa = viter % 2
                            sw = viter % 4
                            if viter >= 2:
                                pv = viter - 2
                                p.wait(S, "sv", svW_l[(c, blk, (pv) // 4, pv % 4)])
                                p.emit(S, lambda e, pet=ets[pv % 4], pn=pv // 4,
                                       psa=pv % 2: e.copy(
                                    out=hstate[:, pet * N + pn:pet * N + pn + 1],
                                    in_=ubuf[:, psa * 512 + 511:psa * 512 + 512]))
                            p.emit(S, lambda e, et=et, ei=ei, n=n, sa=sa: e.activation(
                                out=abar[:, sa * 512:(sa + 1) * 512],
                                in_=dl[:, ei * 512:(ei + 1) * 512], func=AF.Exp,
                                scale=acol[:, et * N + n:et * N + n + 1]))
                            ss_n = s_inc()
                            p.wait(V, "ss", ss_n)
                            if viter >= 4:
                                pv = viter - 4
                                p.wait(V, "st", styacc_l[(c, blk, pv // 4, pv % 4)])
                            p.emit(V, lambda e, ei=ei, nb=nb, sa=sa: e.tensor_tensor(
                                out=ubuf[:, sa * 512:(sa + 1) * 512],
                                in0=dx[:, ei * 512:(ei + 1) * 512],
                                in1=brep[:, nb * 512:(nb + 1) * 512], op=OP.mult))
                            p.emit(V, lambda e, et=et, n=n, sa=sa: e.tensor_tensor_scan(
                                out=ubuf[:, sa * 512:(sa + 1) * 512],
                                data0=abar[:, sa * 512:(sa + 1) * 512],
                                data1=ubuf[:, sa * 512:(sa + 1) * 512],
                                initial=hstate[:, et * N + n:et * N + n + 1],
                                op0=OP.mult, op1=OP.add))
                            p.emit(V, lambda e, nb=nb, sa=sa, sw=sw: e.tensor_tensor(
                                out=wbuf[:, sw * 512:(sw + 1) * 512],
                                in0=ubuf[:, sa * 512:(sa + 1) * 512],
                                in1=crep[:, nb * 512:(nb + 1) * 512], op=OP.mult))
                            svW_l[(c, blk, n, ei)] = v_inc()
                            p.wait(T, "sv", svW_l[(c, blk, n, ei)])
                            styacc_l[(c, blk, n, ei)] = p.emit(
                                T, lambda e, ei=ei, n=n, sw=sw: e.matmul(
                                    pb[ei][:], identb[:],
                                    wbuf[:, sw * 512:(sw + 1) * 512],
                                    start=False, stop=(n == N - 1),
                                    skip_group_check=True), inc=("st", 1))
                            viter += 1
                    p.wait(S, "sv", svW_l[(c, blk, N - 1, 3)])
                    for pv in (4 * N - 2, 4 * N - 1):
                        p.emit(S, lambda e, pet=ets[pv % 4], pn=pv // 4,
                               psa=pv % 2: e.copy(
                            out=hstate[:, pet * N + pn:pet * N + pn + 1],
                            in_=ubuf[:, psa * 512 + 511:psa * 512 + 512]))
                    for ei, et in enumerate(ets):
                        p.wait(V, "st", styacc_l[(c, blk, N - 1, ei)])
                        p.emit(V, lambda e, et=et, ei=ei: e.tensor_tensor(
                            out=xact[:, et * 512:(et + 1) * 512], in0=pb[ei][:],
                            in1=ss_bf[:, et * 512:(et + 1) * 512], op=OP.mult))
                        sv_y2[et] = v_inc()

                if c == 0:
                    dump("dl0", dl[:, 0:512])
                    dump("y2_0", xact[:, 0:512])
                    dump("y2_15", xact[:, 15 * 512:16 * 512])
    
            if PH >= 5:
                p.wait(T, "ss", p.cur("ss"))
                p.wait(T, "sv", sv_y2[ET - 1])
                sv_mix_last = 0
                for hf in range(2 if PH5 >= 1 else 0):
                    for et in range(ET):
                        wo_i = hf * ET + et
                        buf = wo_i % 4
                        p.wait(T, wo_vals[wo_i][0], wo_vals[wo_i][1])
                        p.wait(T, "sv", sv_y2[et])
                        if hf == 1 and et == 0:
                            p.wait(T, "sv", sv_mix_last)
                        for di in range(4 if OF >= 1 else 0):
                            p.emit(T, lambda e, et=et, buf=buf, hf=hf, di=di: e.matmul(
                                pb[di][:],
                                wo_t[:, buf * D + hf * 512 + di * 128:
                                     buf * D + hf * 512 + (di + 1) * 128],
                                xact[:, et * 512:(et + 1) * 512],
                                start=(et == 0), stop=(et == ET - 1)),
                                inc=("st", 1) if di == 3 else None)
                        wo_arr[buf] = p.cur("st")
                        if wo_i + 4 < 2 * ET:
                            load_wo(wo_i + 4)
                    if c == 0 and hf == 0:
                        dump("xq5b", xqT[:, 5 * 512:6 * 512])
                        dump("wqt5", wq_t[:, 5 * 512:6 * 512])
                    p.wait(T, "sv", sv_y2[-1])
                    for dt in range(DTl if OF >= 2 else 0):
                        wq_i = hf * DTl + dt
                        buf = wq_i % 8
                        p.wait(T, wqv[wq_i][0], wqv[wq_i][1])
                        for oi in range(4):
                            p.emit(T, lambda e, dt=dt, buf=buf, oi=oi: e.matmul(
                                pb[4 + oi][:],
                                wq_t[:, buf * 512 + oi * 128:buf * 512 + (oi + 1) * 128],
                                xqT[:, dt * 512:(dt + 1) * 512],
                                start=(dt == 0), stop=(dt == DTl - 1)),
                                inc=("st", 1) if oi == 3 else None)
                        wq_arr[buf] = p.cur("st")
                        if wq_i + 8 < 2 * DTl:
                            load_wq(wq_i + 8)
                    st_half = p.cur("st")
                    p.wait(V, "st", st_half)
                    st_tr_pair = 0
                    for pr in range(2 if OF >= 3 else 0):
                        if pr == 1:
                            p.wait(V, "st", st_tr_pair)
                        for sl in range(2):
                            di = pr * 2 + sl
                            dt_g = hf * 4 + di
                            p.emit(V, lambda e, di=di, dt_g=dt_g, sl=sl: e.scalar_tensor_tensor(
                                out=mxs[:, sl * 512:(sl + 1) * 512], in0=pb[4 + di][:],
                                scalar=fws_c[:, dt_g:dt_g + 1], in1=xsrep[:],
                                op0=OP.mult, op1=OP.mult))
                            p.emit(V, lambda e, di=di, dt_g=dt_g, sl=sl: e.scalar_tensor_tensor(
                                out=mxs[:, sl * 512:(sl + 1) * 512],
                                in0=mxs[:, sl * 512:(sl + 1) * 512],
                                scalar=fbias_c[:, dt_g:dt_g + 1], in1=pb[di][:],
                                op0=OP.add, op1=OP.add))
                        if c == 0 and hf == 0 and pr == 0:
                            dump("ffnpart", mxs[:, 0:512], block=(V,))
                        sv_mx = v_inc()
                        if OF >= 4:
                            p.wait(T, "sv", sv_mx)
                            for sl in range(2):
                                di = pr * 2 + sl
                                for tt in range(TT):
                                    p.emit(T, lambda e, di=di, sl=sl, tt=tt: e.transpose(
                                        pb[di][:, tt * 128:(tt + 1) * 128],
                                        mxs[:, sl * 512 + tt * 128:sl * 512 + (tt + 1) * 128],
                                        identf[:]),
                                        inc=("st", 1) if (sl == 1 and tt == TT - 1) else None)
                            st_tr_pair = p.cur("st")
                        if OF >= 5:
                            p.wait(V, "st", st_tr_pair)
                            for sl in range(2):
                                di = pr * 2 + sl
                                dt_g = hf * 4 + di
                                for tt in range(TT):
                                    xp = xba(par, tt)
                                    p.emit(V, lambda e, xp=xp, tt=tt, dt_g=dt_g: e.tensor_scalar(
                                        out=xp[:, dt_g * 128:(dt_g + 1) * 128],
                                        in0=xp[:, dt_g * 128:(dt_g + 1) * 128],
                                        scalar1=sc(11, tt), scalar2=sc(2, tt),
                                        op0=OP.mult, op1=OP.add))
                            sv_fts = v_inc()
                            p.wait(V, "sv", sv_fts)
                            for sl in range(2):
                                di = pr * 2 + sl
                                dt_g = hf * 4 + di
                                for tt in range(TT):
                                    xp = xba(par, tt)
                                    p.emit(V, lambda e, xp=xp, di=di, tt=tt, dt_g=dt_g: e.tensor_tensor(
                                        out=xp[:, dt_g * 128:(dt_g + 1) * 128],
                                        in0=xp[:, dt_g * 128:(dt_g + 1) * 128],
                                        in1=pb[di][:, tt * 128:(tt + 1) * 128], op=OP.add))
                            sv_mix_last = v_inc()
                sv_final = p.cur("sv")
                p.wait(SY, "sv", sv_final)
                for tt in range(TT):
                    p.emit(SY, lambda e, tt=tt, c=c, par=par: e.dma_start(
                        out=y_out[c * CH + tt * 128:c * CH + (tt + 1) * 128, :],
                        in_=xba(par, tt)), inc=("sox", 16))

        p.wait(SY, "sox", p.cur("sox"))
        for s in ("sv", "ss", "st", "gq"):
            p.wait(SY, s, p.cur(s))
        for s in ("sv", "ss", "st", "gq"):
            p.wait(SY, s, p.cur(s))

        with nc.Block() as block:
            @block.sync
            def _(sync):
                for fn in p.ops["sync"]:
                    fn(sync)

            @block.gpsimd
            def _(gpsimd):
                for fn in p.ops["gpsimd"]:
                    fn(gpsimd)

            @block.vector
            def _(vector):
                for fn in p.ops["vector"]:
                    fn(vector)

            @block.scalar
            def _(scalar):
                for fn in p.ops["scalar"]:
                    fn(scalar)

            @block.tensor
            def _(tensor):
                for fn in p.ops["tensor"]:
                    fn(tensor)

    return nc


def host_prep(inputs):
    g = lambda k: np.ascontiguousarray(inputs[k])
    resid = g("L0_attn_resid")
    skip_w = g("L0_attn_skip_w")
    in_w = g("L0_attn_in_w")
    out_w = g("L0_attn_out_w")
    wq = g("L0_ffn_weight_q")

    def colmaj(v, ntile):
        return np.ascontiguousarray(v.reshape(ntile, 128).T)

    shared = {
        "w_in": np.ascontiguousarray(in_w.T),
        "w_skip": np.ascontiguousarray(skip_w.T.reshape(D, ET, 128).transpose(1, 0, 2)),
        "w_out": np.ascontiguousarray(out_w.T.reshape(ET, 128, D)),
        "w_q": np.ascontiguousarray(wq.T),
        "wd1t": np.ascontiguousarray(
            g("L0_attn_wd1").T.reshape(ET, 128, DT).transpose(1, 0, 2)
            .reshape(128, ET * DT)),
        "wd2t": np.ascontiguousarray(g("L0_attn_wd2").T),
        "wbt": np.ascontiguousarray(
            g("L0_attn_wb").T.reshape(ET, 128, N).transpose(1, 0, 2)
            .reshape(128, ET * N)),
        "wct": np.ascontiguousarray(
            g("L0_attn_wc").T.reshape(ET, 128, N).transpose(1, 0, 2)
            .reshape(128, ET * N)),
        "alog": np.ascontiguousarray(
            g("L0_attn_A_log").reshape(ET, 128, N).transpose(1, 0, 2)
            .reshape(128, ET * N)),
        "normw": colmaj(g("L0_attn_norm_w"), DTl),
        "convw": np.ascontiguousarray(
            g("L0_attn_conv_w").reshape(ET, 128, K).transpose(1, 0, 2)
            .reshape(128, ET * K)),
        "convb": colmaj(g("L0_attn_conv_b"), ET),
        "wd2b": colmaj(g("L0_attn_wd2_b"), ET),
        "wdvec": colmaj(g("L0_attn_W_D"), ET),
        "fbias": colmaj(g("L0_ffn_bias"), DTl),
        "fws": colmaj(g("L0_ffn_weight_scale"), DTl),
    }
    return [dict(shared, x=np.ascontiguousarray(resid[b])) for b in range(8)]


def kernel(**inputs):
    in_maps = host_prep(inputs)
    nc = build_core()
    res = run_bass_kernel_spmd(nc, in_maps, core_ids=list(range(8)))
    return np.stack([r["y"] for r in res.results], axis=0).astype(np.float32)
